# version 24
# baseline (speedup 1.0000x reference)
import contextlib
import math

import numpy as np
import ml_dtypes

import concourse.bass as bass
import concourse.mybir as mybir
from concourse.bass_utils import run_bass_kernel_spmd

F32 = mybir.dt.float32
BF16 = mybir.dt.bfloat16
FP8 = mybir.dt.float8e4
I8 = mybir.dt.int8
AF = mybir.ActivationFunctionType
ALU = mybir.AluOpType
AX = mybir.AxisListType

T = 2048
TC = 16
D = 1024
DC = 8
H = 16
DK = 64
FF = 4096

B3 = float(np.float32(1.5 * 2.0 ** 52))
B3S = float(np.float32(1.5 * 2.0 ** 43))
B3T = float(np.float32(1.5 * 2.0 ** 36))
B5 = float(np.float32(1.5 * 2.0 ** 53))
LH = float(np.float32(math.log(2.0)))
LL = float(np.float32(math.log(2.0) - np.float64(np.float32(math.log(2.0)))))
INVLN2 = 1.0 / math.log(2.0)
TIE_THR = float(np.float32((0.5 - 2.0 ** -20) ** 2))

ENGINES = ("sync", "vector", "scalar", "tensor", "gpsimd")


def bx(t, p0, p1, f0, f1):
    return (t, p0, p1, f0, f1)


class Prog:

    def __init__(self, nc):
        self.nc = nc
        self.ops = {e: [] for e in ENGINES}
        self.cnt = {}
        self.semh = {}
        self.dw = {e: [] for e in ENGINES}
        self.dr = {e: [] for e in ENGINES}

    def sem(self, es, name):
        self.semh[name] = es.enter_context(self.nc.semaphore(name))
        self.cnt[name] = 0

    @staticmethod
    def _ov(a, b):
        return (a[0] is b[0]) and a[1] < b[2] and b[1] < a[2] and a[3] < b[4] and b[3] < a[4]

    def wait(self, e, sname, val):
        h = self.semh[sname]
        self.ops[e].append(lambda eng: eng.wait_ge(h, val))

    def drain(self, e):
        self.ops[e].append(lambda eng: eng.drain())
        self.dw[e].clear()
        self.dr[e].clear()

    def op(self, e, fn, reads=(), writes=(), inc=None, drainable=True):
        if drainable:
            hit = any(any(self._ov(r, w) for w in self.dw[e]) for r in reads)
            if not hit:
                hit = any(any(self._ov(w, x) for x in self.dw[e]) or
                          any(self._ov(w, x) for x in self.dr[e]) for w in writes)
            if hit:
                self.drain(e)
        if inc is not None:
            s, n = inc
            h = self.semh[s]
            self.ops[e].append(lambda eng: fn(eng).then_inc(h, n))
            self.cnt[s] += n
        else:
            self.ops[e].append(lambda eng: fn(eng))
        self.dr[e].extend(reads)
        self.dw[e].extend(writes)
        return self.cnt.get(inc[0]) if inc else None

    def dma(self, out_ap, in_ap, sem="LW", eng="sync", wait=None, slow=False):
        if wait is not None:
            self.wait(eng, wait[0], wait[1])
        h = self.semh[sem]
        if slow:
            self.ops[eng].append(
                lambda e: e.dma_start(out=out_ap, in_=in_ap,
                                      allow_slow_non_contiguous=True).then_inc(h, 16))
        else:
            self.ops[eng].append(lambda e: e.dma_start(out=out_ap, in_=in_ap).then_inc(h, 16))
        self.cnt[sem] += 16
        return self.cnt[sem]

    def dmat(self, out_ap, in_ap, sem="LW", eng="sync", wait=None):
        if wait is not None:
            self.wait(eng, wait[0], wait[1])
        h = self.semh[sem]
        self.ops[eng].append(lambda e: e.dma_start_transpose(out_ap, in_ap).then_inc(h, 16))
        self.cnt[sem] += 16
        return self.cnt[sem]

    def mms(self, mmlist, sem="CT"):
        h = self.semh[sem]
        n = len(mmlist)
        for i, (o, l, r, st, sp) in enumerate(mmlist):
            if i == n - 1:
                self.ops["tensor"].append(
                    lambda e, o=o, l=l, r=r, st=st, sp=sp: e.matmul(
                        o, l, r, start=st, stop=sp).then_inc(h, 1))
            else:
                self.ops["tensor"].append(
                    lambda e, o=o, l=l, r=r, st=st, sp=sp: e.matmul(o, l, r, start=st, stop=sp))
        self.cnt[sem] += 1
        return self.cnt[sem]

    def trans(self, tlist, sem="CT"):
        h = self.semh[sem]
        n = len(tlist)
        for i, (o, a, idn) in enumerate(tlist):
            if i == n - 1:
                self.ops["tensor"].append(
                    lambda e, o=o, a=a, idn=idn: e.transpose(o, a, idn).then_inc(h, 1))
            else:
                self.ops["tensor"].append(lambda e, o=o, a=a, idn=idn: e.transpose(o, a, idn))
        self.cnt[sem] += 1
        return self.cnt[sem]


def build_program(parts=4):
    nc = bass.Bass("TRN2", target_bir_lowering=False)

    x_in = nc.dram_tensor("x_in", [T, D], F32, kind="ExternalInput")
    out_t = nc.dram_tensor("out", [T, D], F32, kind="ExternalOutput")
    ident_f = nc.dram_tensor("ident_f", [128, 128], F32, kind="ExternalInput")
    ident_b = nc.dram_tensor("ident_b", [128, 128], BF16, kind="ExternalInput")
    diagm = nc.dram_tensor("diagm", [128, 128], F32, kind="ExternalInput")

    wq0 = nc.dram_tensor("L0_attn_Wq", [H, D, DK], F32, kind="ExternalInput")
    bq0 = nc.dram_tensor("L0_attn_bq", [H, DK], F32, kind="ExternalInput")
    wk0 = nc.dram_tensor("L0_attn_Wk", [H, D, DK], F32, kind="ExternalInput")
    bk0 = nc.dram_tensor("L0_attn_bk", [H, DK], F32, kind="ExternalInput")
    wv0 = nc.dram_tensor("L0_attn_Wv", [H, D, DK], F32, kind="ExternalInput")
    bv0 = nc.dram_tensor("L0_attn_bv", [H, DK], F32, kind="ExternalInput")
    wo0 = nc.dram_tensor("L0_attn_Wo", [D, D], F32, kind="ExternalInput")
    bo0 = nc.dram_tensor("L0_attn_bo", [D], F32, kind="ExternalInput")
    w1 = nc.dram_tensor("L0_ffn_W1", [FF, D], F32, kind="ExternalInput")
    b1 = nc.dram_tensor("L0_ffn_b1", [FF], F32, kind="ExternalInput")
    w2 = nc.dram_tensor("L0_ffn_W2", [D, FF], F32, kind="ExternalInput")
    b2 = nc.dram_tensor("L0_ffn_b2", [D], F32, kind="ExternalInput")
    wqkv1 = nc.dram_tensor("L1_attn_W_qkv", [D, 3 * D], F32, kind="ExternalInput")
    bqkv1 = nc.dram_tensor("L1_attn_b_qkv", [3 * D], F32, kind="ExternalInput")
    wout1 = nc.dram_tensor("L1_attn_W_out", [D, D], F32, kind="ExternalInput")
    bout1 = nc.dram_tensor("L1_attn_b_out", [D], F32, kind="ExternalInput")
    kern1 = nc.dram_tensor("L1_ffn_kernel", [D, D], BF16, kind="ExternalInput")
    sin1 = nc.dram_tensor("L1_ffn_input_scale", [1], F32, kind="ExternalInput")
    sk1 = nc.dram_tensor("L1_ffn_kernel_scale", [1], F32, kind="ExternalInput")
    bias1 = nc.dram_tensor("L1_ffn_bias", [D], BF16, kind="ExternalInput")

    w1t_d = nc.dram_tensor("w1t_d", [FF, D], BF16)
    w1T_d = nc.dram_tensor("w1T_d", [16, 128, 2048], BF16)
    w2t_d = nc.dram_tensor("w2t_d", [D, FF], BF16)
    w2T_d = nc.dram_tensor("w2T_d", [FF, D], BF16)

    P = Prog(nc)

    with contextlib.ExitStack() as g:
        def sb(nm, shape, dt, es=g):
            return es.enter_context(nc.sbuf_tensor(nm, shape, dt))

        x_res = sb("x_res", [128, TC * D], F32)
        idf = sb("idf", [128, 128], F32)
        idb = sb("idb", [128, 128], BF16)
        dmask = sb("dmask", [128, 128], F32)
        onescol = sb("onescol", [1, 128], F32)
        ones128 = sb("ones128", [128, 1], BF16)
        smalls = sb("smalls", [128, 64], F32)
        st_sx = sb("st_sx", [128, TC], F32)
        st_sq = sb("st_sq", [128, TC], F32)
        st_m = sb("st_m", [128, TC], F32)
        st_w = sb("st_w", [128, TC], F32)
        btile = sb("btile", [128, D], F32)
        jnk = btile
        cb_eps = sb("cb_eps", [128, 1], F32)
        cb_tiny = sb("cb_tiny", [128, 1], F32)

        for s in ("LW", "LX", "GX", "LZ", "ST", "TS", "TS2", "LY", "CT", "CV", "CS",
                  "LA0", "LA1", "LB0", "LB1"):
            P.sem(g, s)

        psA = g.enter_context(nc.psum_tensor("psA", [128, 2048], F32))
        psB = g.enter_context(nc.psum_tensor("psB", [128, 2048], F32))

        P.dma(idf[:, :], ident_f.ap(), sem="LX")
        P.dma(idb[:, :], ident_b.ap(), sem="LX")
        P.dma(dmask[:, :], diagm.ap(), sem="LX")
        P.dma(bass.AP(x_res, 0, [[TC * D, 128], [D, TC], [1, D]]),
              bass.AP(x_in, 0, [[D, 128], [128 * D, TC], [1, D]]), sem="LX")
        x_loaded = P.cnt["LX"]
        P.op("vector", lambda e: e.memset(onescol[:, :], 1.0), writes=[bx(onescol, 0, 1, 0, 128)])
        P.op("vector", lambda e: e.memset(ones128[:, :], 1.0), writes=[bx(ones128, 0, 128, 0, 1)])
        P.op("vector", lambda e: e.memset(cb_eps[:, :], 1e-5), writes=[bx(cb_eps, 0, 128, 0, 1)])
        P.op("vector", lambda e: e.memset(cb_tiny[:, :], 1e-37), writes=[bx(cb_tiny, 0, 128, 0, 1)])
        P.op("vector", lambda e: e.memset(smalls[:, :], 0.0), writes=[bx(smalls, 0, 128, 0, 64)])
        P.wait("vector", "LX", x_loaded)
        P.wait("scalar", "LX", x_loaded)
        P.wait("tensor", "LX", x_loaded)

        def ln_inplace(wait_cv):
            P.wait("scalar", "CV", wait_cv)
            for tc in range(TC):
                xa = x_res[:, tc * D:(tc + 1) * D]
                rb = [bx(x_res, 0, 128, tc * D, (tc + 1) * D)]
                P.op("scalar", lambda e, xa=xa, tc=tc: e.activation(
                    jnk[:, :], xa, AF.Identity, accum_out=st_sx[:, tc:tc + 1]),
                    reads=rb, writes=[bx(jnk, 0, 128, 0, D), bx(st_sx, 0, 128, tc, tc + 1)])
                P.op("scalar", lambda e, xa=xa, tc=tc: e.activation(
                    jnk[:, :], xa, AF.Square, accum_out=st_sq[:, tc:tc + 1]),
                    reads=rb, writes=[bx(jnk, 0, 128, 0, D), bx(st_sq, 0, 128, tc, tc + 1)])
            self_d = P.op("scalar", lambda e: e.activation(jnk[:, 0:1], jnk[:, 0:1], AF.Identity),
                          reads=[bx(jnk, 0, 128, 0, 1)], writes=[bx(jnk, 0, 128, 0, 1)],
                          inc=("CS", 1))
            P.wait("vector", "CS", self_d)
            P.op("vector", lambda e: e.tensor_scalar(st_m[:, :], st_sx[:, :], 1.0 / D, None, ALU.mult),
                 reads=[bx(st_sx, 0, 128, 0, TC)], writes=[bx(st_m, 0, 128, 0, TC)])
            P.op("vector", lambda e: e.tensor_tensor(st_w[:, :], st_m[:, :], st_m[:, :], ALU.mult),
                 reads=[bx(st_m, 0, 128, 0, TC)], writes=[bx(st_w, 0, 128, 0, TC)])
            cv_var = P.op("vector", lambda e: e.scalar_tensor_tensor(
                st_w[:, :], st_sq[:, :], 1.0 / D, st_w[:, :], ALU.mult, ALU.subtract),
                reads=[bx(st_sq, 0, 128, 0, TC), bx(st_w, 0, 128, 0, TC)],
                writes=[bx(st_w, 0, 128, 0, TC)], inc=("CV", 1))
            P.wait("scalar", "CV", cv_var)
            P.op("scalar", lambda e: e.activation(st_w[:, :], st_w[:, :], AF.Ln,
                                                  bias=cb_eps[:, 0:1], scale=1.0),
                 reads=[bx(st_w, 0, 128, 0, TC)], writes=[bx(st_w, 0, 128, 0, TC)])
            cs_w = P.op("scalar", lambda e: e.activation(st_w[:, :], st_w[:, :], AF.Exp,
                                                         bias=0.0, scale=-0.5),
                        reads=[bx(st_w, 0, 128, 0, TC)], writes=[bx(st_w, 0, 128, 0, TC)],
                        inc=("CS", 1))
            P.wait("vector", "CS", cs_w)
            done = None
            for tc in range(TC):
                b = bx(x_res, 0, 128, tc * D, (tc + 1) * D)
                done = P.op("vector", lambda e, tc=tc: e.tensor_scalar(
                    x_res[:, tc * D:(tc + 1) * D], x_res[:, tc * D:(tc + 1) * D],
                    st_m[:, tc:tc + 1], st_w[:, tc:tc + 1], ALU.subtract, ALU.mult),
                    reads=[b, bx(st_m, 0, 128, tc, tc + 1), bx(st_w, 0, 128, tc, tc + 1)],
                    writes=[b], inc=("CV", 1))
            return done

        def build_xT(xT_hi, xT_lo, wait_cv, alt=True):
            par_cv = {0: wait_cv, 1: wait_cv}
            last = None
            for dc in range(DC):
                par = dc % 2 if alt else 0
                ps = psB if par == 0 else psA
                P.wait("tensor", "CV", par_cv[par])
                tl = [(ps[:, tc * 128:(tc + 1) * 128],
                       x_res[:, tc * D + dc * 128: tc * D + (dc + 1) * 128],
                       idf[:, :]) for tc in range(TC)]
                ct = P.trans(tl)
                P.wait("vector", "CT", ct)
                hi_b = bx(xT_hi, 0, 128, dc * T, (dc + 1) * T)
                lo_b = bx(xT_lo, 0, 128, dc * T, (dc + 1) * T)
                P.op("vector", lambda e, dc=dc, ps=ps: e.tensor_copy(
                    xT_hi[:, dc * T:(dc + 1) * T], ps[:, :]), writes=[hi_b])
                last = P.op("vector", lambda e, dc=dc, ps=ps: e.tensor_tensor(
                    xT_lo[:, dc * T:(dc + 1) * T], ps[:, :],
                    xT_hi[:, dc * T:(dc + 1) * T], ALU.subtract),
                    reads=[hi_b], writes=[lo_b], inc=("CV", 1))
                par_cv[par] = last
            return last

        def build_xT_f32(xTf, wait_cv):
            par_cv = {0: wait_cv, 1: wait_cv}
            last = None
            for dc in range(DC):
                par = dc % 2
                ps = psB if par == 0 else psA
                P.wait("tensor", "CV", par_cv[par])
                tl = [(ps[:, tc * 128:(tc + 1) * 128],
                       x_res[:, tc * D + dc * 128: tc * D + (dc + 1) * 128],
                       idf[:, :]) for tc in range(TC)]
                ct = P.trans(tl)
                P.wait("vector", "CT", ct)
                last = P.op("vector", lambda e, dc=dc, ps=ps: e.tensor_copy(
                    xTf[:, dc * T:(dc + 1) * T], ps[:, :]),
                    writes=[bx(xTf, 0, 128, dc * T, (dc + 1) * T)], inc=("CV", 1))
                par_cv[par] = last
            return last

        def attn_layer(layer, xTf, wait_cv):
            causal = (layer == 0)
            L = str(layer)
            with contextlib.ExitStack() as S:
                wp = [sb("a_wp0" + L, [128, 1024], F32, S),
                      sb("a_wp1" + L, [128, 1024], F32, S)]
                wohf = [sb("a_woh0" + L, [64, 1024], F32, S),
                        sb("a_woh1" + L, [64, 1024], F32, S)]
                kf = sb("a_kf" + L, [128, T], F32, S)
                qf = sb("a_qf" + L, [128, T], F32, S)
                Vst = [sb("a_Vst0" + L, [128, TC * 65], F32, S),
                       sb("a_Vst1" + L, [128, TC * 65], F32, S)]
                Ef = sb("a_Ef" + L, [128, 1024], F32, S)
                Efh = [sb("a_Efh0" + L, [128, 1024], F32, S),
                       sb("a_Efh1" + L, [128, 1024], F32, S)]
                cxf = [sb("a_cx0" + L, [64, T], F32, S),
                       sb("a_cx1" + L, [64, T], F32, S)]
                bqt = sb("a_bqt" + L, [128, 8], F32, S)
                bkt = sb("a_bkt" + L, [128, 8], F32, S)
                bvt = sb("a_bvt" + L, [128, 8], F32, S)

                P.op("vector", lambda e: e.memset(smalls[64:65, 0:64], 1.0),
                     writes=[bx(smalls, 64, 65, 0, 64)])

                cv_entry = P.cnt["CV"]
                cs_entry = P.cnt["CS"]
                P.wait("sync", "CV", P.cnt["CV"])
                P.wait("sync", "LW", P.cnt["LW"])
                P.wait("sync", "ST", P.cnt["ST"])
                P.wait("sync", "TS", P.cnt["TS"])
                for hp in range(8):
                    if layer == 0:
                        P.dma(bqt[:, hp:hp + 1], bass.AP(bq0, 2 * hp * DK, [[DK, 2], [1, DK]]), slow=True)
                        P.dma(bkt[:, hp:hp + 1], bass.AP(bk0, 2 * hp * DK, [[DK, 2], [1, DK]]), slow=True)
                    else:
                        P.dma(bqt[:, hp:hp + 1], bass.AP(bqkv1, 2 * hp * 192, [[192, 2], [1, DK]]), slow=True)
                        P.dma(bkt[:, hp:hp + 1], bass.AP(bqkv1, 2 * hp * 192 + DK, [[192, 2], [1, DK]]), slow=True)
                bsrc = bo0 if layer == 0 else bout1
                P.wait("sync", "CS", P.cnt["CS"])
                bias_ld = P.dma(btile[:, :], bass.AP(bsrc, 0, [[0, 128], [1, D]]), slow=True)
                P.wait("scalar", "LW", bias_ld)
                P.wait("vector", "LW", bias_ld)
                bt_done = None
                for tc in range(TC):
                    b = bx(x_res, 0, 128, tc * D, (tc + 1) * D)
                    bt_done = P.op("vector", lambda e, tc=tc: e.tensor_tensor(
                        x_res[:, tc * D:(tc + 1) * D], x_res[:, tc * D:(tc + 1) * D],
                        btile[:, :], ALU.add),
                        reads=[b, bx(btile, 0, 128, 0, D)], writes=[b], inc=("CV", 1))
                P.wait("tensor", "CV", bt_done)

                cs_exp_h = {0: 0, 1: 0}
                ct_pv_h = {0: 0, 1: 0}
                pair_free = {0: 0, 1: 0}
                add_done = {0: bt_done, 1: bt_done}
                ct_outproj_prev = 0
                cv_ctx_prev = cv_entry
                cs_psA_prev = cs_entry
                cv_msk = 0
                ct_pv = 0

                for hp in range(8):
                    def load_w(ti, t, par, hp=hp):
                        if pair_free[par] == 0:
                            P.wait("sync", "LW", P.cnt["LW"])
                        P.wait("sync", "CT", pair_free[par])
                        lw = None
                        for h2 in range(2):
                            if layer == 0:
                                srcw = {"q": wq0, "k": wk0, "v": wv0}[t]
                                in_ap = bass.AP(srcw, (2 * hp + h2) * D * DK,
                                                [[DK, 128], [128 * DK, DC], [1, DK]])
                            else:
                                in_ap = bass.AP(wqkv1, (2 * hp + h2) * 192 + ti * DK,
                                                [[3 * D, 128], [128 * 3 * D, DC], [1, DK]])
                            wst_ap = bass.AP(wp[par], h2 * 64, [[1024, 128], [128, DC], [1, 64]])
                            lw = P.dma(wst_ap, in_ap, slow=True)
                        return lw

                    def proj_fm(ps, par, lw, gate_cv, gate_cs):
                        P.wait("tensor", "CS", gate_cs)
                        P.wait("tensor", "CV", gate_cv)
                        P.wait("tensor", "LW", lw)
                        mml = []
                        for dc in range(DC):
                            for blk in range(4):
                                c0 = blk * 512
                                mml.append((ps[:, c0:c0 + 512],
                                            wp[par][:, dc * 128:(dc + 1) * 128],
                                            xTf[:, dc * T + c0: dc * T + c0 + 512],
                                            dc == 0, dc == DC - 1))
                        return P.mms(mml)

                    load_w(0, "q", 0)
                    load_w(1, "k", 1)
                    ct_q = proj_fm(psA, 0, P.cnt["LW"], cv_ctx_prev, cs_psA_prev)
                    pair_free[0] = ct_q
                    P.wait("scalar", "CT", ct_q)
                    cs_qb = P.op("scalar", lambda e, hp=hp: e.activation(
                        qf[:, :], psA[:, :], AF.Identity, bias=bqt[:, hp:hp + 1], scale=1.0),
                        writes=[bx(qf, 0, 128, 0, T)], inc=("CS", 1))
                    load_w(2, "v", 0)
                    ct_k = proj_fm(psB, 1, P.cnt["LW"], max(add_done[0], add_done[1]),
                                   cs_psA_prev)
                    pair_free[1] = ct_k
                    P.wait("scalar", "CT", ct_k)
                    cs_k = P.op("scalar", lambda e, hp=hp: e.activation(
                        kf[:, :], psB[:, :], AF.Identity, bias=bkt[:, hp:hp + 1], scale=1.0),
                        writes=[bx(kf, 0, 128, 0, T)], inc=("CS", 1))

                    wsrc = wo0 if layer == 0 else wout1
                    P.wait("sync", "LW", P.cnt["LW"])
                    P.wait("sync", "CT", ct_outproj_prev)
                    lw_wo = None
                    for h2 in range(2):
                        lw_wo = P.dma(bass.AP(wohf[h2], 0, [[1024, 64], [1, D]]),
                                      bass.AP(wsrc, (hp * 128 + h2 * 64) * D, [[D, 64], [1, D]]))

                    if layer == 0:
                        bv_ld = P.dma(bvt[:, hp:hp + 1], bass.AP(bv0, 2 * hp * DK, [[DK, 2], [1, DK]]),
                                      sem="LX", wait=("CV", P.cnt["CV"]), slow=True)
                    else:
                        bv_ld = P.dma(bvt[:, hp:hp + 1], bass.AP(bqkv1, 2 * hp * 192 + 2 * DK,
                                                                 [[192, 2], [1, DK]]),
                                      sem="LX", wait=("CV", P.cnt["CV"]), slow=True)

                    ct_v = proj_fm(psA, 0, P.cnt["LW"], cv_ctx_prev, cs_qb)
                    pair_free[0] = ct_v

                    P.wait("scalar", "CT", ct_v)
                    P.wait("scalar", "LX", bv_ld)
                    P.wait("scalar", "CV", cv_ctx_prev)
                    cs_v1 = P.op("scalar", lambda e, hp=hp: e.activation(
                        Ef[:, :], psA[:, 0:1024], AF.Identity, bias=bvt[:, hp:hp + 1], scale=1.0),
                        writes=[bx(Ef, 0, 128, 0, 1024)], inc=("CS", 1))
                    P.wait("tensor", "CS", cs_v1)
                    tl = [(psB[:, i * 128:(i + 1) * 128],
                           Ef[:, i * 128:(i + 1) * 128], idf[:, :]) for i in range(8)]
                    ct_t1 = P.trans(tl)
                    P.wait("scalar", "CT", ct_t1)
                    cs_v2 = P.op("scalar", lambda e, hp=hp: e.activation(
                        Ef[:, :], psA[:, 1024:2048], AF.Identity, bias=bvt[:, hp:hp + 1], scale=1.0),
                        writes=[bx(Ef, 0, 128, 0, 1024)], inc=("CS", 1))
                    P.wait("tensor", "CS", cs_v2)
                    tl = [(psB[:, 1024 + i * 128:1024 + (i + 1) * 128],
                           Ef[:, i * 128:(i + 1) * 128], idf[:, :]) for i in range(8)]
                    ct_t2 = P.trans(tl)

                    P.wait("vector", "CT", ct_t2)
                    cv_vst = None
                    for h2 in range(2):
                        r0 = h2 * 64
                        P.op("vector", lambda e, r0=r0, h2=h2: e.tensor_copy(
                            bass.AP(Vst[h2], 0, [[TC * 65, 128], [65, TC], [1, 64]]),
                            bass.AP(psB, r0, [[2048, 128], [128, TC], [1, 64]])),
                            writes=[bx(Vst[h2], 0, 128, 0, TC * 65)])
                        cv_vst = P.op("vector", lambda e, h2=h2: e.memset(
                            bass.AP(Vst[h2], 64, [[TC * 65, 128], [65, TC], [1, 1]]), 1.0),
                            writes=[bx(Vst[h2], 0, 128, 0, TC * 65)], inc=("CV", 1))

                    P.wait("tensor", "CV", cv_vst)
                    for h2 in range(2):
                        r0, r1 = h2 * 64, (h2 + 1) * 64
                        for j in range(TC):
                            qb0 = 512 * (j // 4) if causal else 0
                            live0 = j * 128 if causal else 0
                            halves = [hh for hh in range(2)
                                      if max(qb0, hh * 1024) < (hh + 1) * 1024]
                            ct_sc_h = {}
                            for hh in halves:
                                lo_a = max(qb0, hh * 1024)
                                hi_a = (hh + 1) * 1024
                                P.wait("tensor", "CS", cs_exp_h[hh])
                                mml = [(psB[:, c0:c0 + 512],
                                        kf[r0:r1, j * 128:(j + 1) * 128],
                                        qf[r0:r1, c0:c0 + 512], True, True)
                                       for c0 in range(lo_a, hi_a, 512)]
                                ct_sc_h[hh] = P.mms(mml)
                            cv_mask = 0
                            if causal:
                                dh = (j * 128) // 1024
                                P.wait("vector", "CT", ct_sc_h[dh])
                                cv_mask = P.op("vector", lambda e, j=j: e.tensor_tensor(
                                    psB[:, j * 128:(j + 1) * 128],
                                    psB[:, j * 128:(j + 1) * 128],
                                    dmask[:, :], ALU.add),
                                    reads=[bx(dmask, 0, 128, 0, 128)],
                                    writes=[bx(psB, 0, 128, j * 128, (j + 1) * 128)],
                                    inc=("CV", 1))
                            for hh in halves:
                                lo_a = max(qb0, hh * 1024)
                                hi_a = (hh + 1) * 1024
                                lv_a = max(live0, lo_a)
                                lo_l, lv_l = lo_a - hh * 1024, lv_a - hh * 1024
                                Eh = Efh[hh]
                                if causal and (j * 128) // 1024 == hh:
                                    P.wait("scalar", "CV", cv_mask)
                                P.wait("scalar", "CT", ct_sc_h[hh])
                                P.wait("scalar", "CT", ct_pv_h[hh])
                                cs_exp_h[hh] = P.op("scalar", lambda e, lv_a=lv_a, hi_a=hi_a,
                                                    lv_l=lv_l, Eh=Eh: e.activation(
                                    Eh[:, lv_l:1024], psB[:, lv_a:hi_a], AF.Exp,
                                    bias=0.0, scale=0.125),
                                    writes=[bx(Eh, 0, 128, lv_l, 1024)], inc=("CS", 1))
                                if lv_l > lo_l:
                                    P.wait("vector", "CT", ct_pv_h[hh])
                                    cv_msk = P.op("vector", lambda e, lo_l=lo_l, lv_l=lv_l,
                                                  Eh=Eh: e.memset(Eh[:, lo_l:lv_l], 0.0),
                                                  writes=[bx(Eh, 0, 128, lo_l, lv_l)],
                                                  inc=("CV", 1))
                                    P.wait("tensor", "CV", cv_msk)
                                P.wait("tensor", "CS", cs_exp_h[hh])
                                va = Vst[h2][:, j * 65:(j + 1) * 65]
                                mml = []
                                for c0 in range(lo_a, hi_a, 512):
                                    qbi = c0 // 512
                                    lastj = min(4 * qbi + 3, TC - 1) if causal else TC - 1
                                    cl = c0 - hh * 1024
                                    mml.append((psA[0:65, c0:c0 + 512], va,
                                                Eh[:, cl:cl + 512],
                                                j == 0, j == lastj))
                                ct_pv_h[hh] = P.mms(mml)
                                ct_pv = ct_pv_h[hh]
                        P.wait("vector", "CT", ct_pv)
                        P.wait("vector", "CT", ct_outproj_prev)
                        cv_ctx = None
                        P.wait("scalar", "CT", ct_pv)
                        for hh in range(2):
                            a0 = hh * 1024
                            P.wait("scalar", "CT", P.cnt["CT"])
                            P.op("scalar", lambda e, a0=a0: e.activation(
                                Ef[64:65, 0:1024], psA[64:65, a0:a0 + 1024], AF.Ln),
                                writes=[bx(Ef, 64, 65, 0, 1024)])
                            cs_re = P.op("scalar", lambda e: e.activation(
                                Ef[64:65, 0:1024], Ef[64:65, 0:1024], AF.Exp,
                                bias=0.0, scale=-1.0),
                                reads=[bx(Ef, 64, 65, 0, 1024)],
                                writes=[bx(Ef, 64, 65, 0, 1024)], inc=("CS", 1))
                            cv_ecp = P.op("vector", lambda e, a0=a0: e.tensor_copy(
                                Ef[0:64, 0:1024], psA[0:64, a0:a0 + 1024]),
                                writes=[bx(Ef, 0, 64, 0, 1024)], inc=("CV", 1))
                            P.wait("tensor", "CS", cs_re)
                            P.wait("tensor", "CV", cv_ecp)
                            mml = [(psA[0:64, a0 + c0:a0 + c0 + 512], smalls[64:65, 0:64],
                                    Ef[64:65, c0:c0 + 512], True, True) for c0 in (0, 512)]
                            ct_bc = P.mms(mml)
                            P.wait("vector", "CT", ct_bc)
                            cv_ctx = P.op("vector", lambda e, h2=h2, a0=a0: e.tensor_tensor(
                                cxf[h2][0:64, a0:a0 + 1024], Ef[0:64, 0:1024],
                                psA[0:64, a0:a0 + 1024], ALU.mult),
                                reads=[bx(Ef, 0, 64, 0, 1024)],
                                writes=[bx(cxf[h2], 0, 64, a0, a0 + 1024)], inc=("CV", 1))
                        P.wait("tensor", "CV", cv_ctx)

                    P.wait("tensor", "CV", cv_ctx)
                    P.wait("tensor", "CS", P.cnt["CS"])
                    P.wait("tensor", "LW", P.cnt["LW"])
                    for tc in range(TC):
                        reg0 = (tc % 2) * 1024
                        P.wait("tensor", "CV", add_done[tc % 2])
                        mml = []
                        for h2 in range(2):
                            for c0 in range(0, 1024, 512):
                                mml.append((psB[:, reg0 + c0:reg0 + c0 + 512],
                                            cxf[h2][0:64, tc * 128:(tc + 1) * 128],
                                            wohf[h2][0:64, c0:c0 + 512],
                                            h2 == 0, h2 == 1))
                        ct_op = P.mms(mml)
                        P.wait("vector", "CT", ct_op)
                        b = bx(x_res, 0, 128, tc * D, (tc + 1) * D)
                        add_done[tc % 2] = P.op("vector", lambda e, tc=tc, reg0=reg0: e.tensor_tensor(
                            x_res[:, tc * D:(tc + 1) * D], x_res[:, tc * D:(tc + 1) * D],
                            psB[:, reg0:reg0 + 1024], ALU.add),
                            reads=[b], writes=[b], inc=("CV", 1))
                        ct_outproj_prev = ct_op
                    cv_ctx_prev = cv_ctx
                    cs_psA_prev = P.cnt["CS"]
                return ln_inplace(P.cnt["CV"])

        def ternarize_weights(S):
            if True:
                wstg = [sb("t_wstg%d" % i, [128, 1024], F32, S) for i in range(4)]
                bfb = [sb("t_bfb0", [128, 1024], BF16, S), sb("t_bfb1", [128, 1024], BF16, S)]
                geb = sb("t_geb", [128, 1024], F32, S)
                hbc = sb("t_hbc", [128, 1], F32, S)
                hbn = sb("t_hbn", [128, 1], F32, S)
                tbb = [sb("t_tb0", [128, 1024], BF16, S), sb("t_tb1", [128, 1024], BF16, S)]

                P.wait("sync", "LW", P.cnt["LW"])
                P.wait("sync", "CV", P.cnt["CV"])
                sts = []
                st_tb = {0: 0, 1: 0}

                def transpose_group(wi, k, eng, sls, sst, ssem):
                    gate = P.cnt[ssem]
                    for j in range(8):
                        b = j % 2
                        if wi == 0:
                            dc, r0 = j, k * 1024
                            in_ap = bass.AP(w1t_d, dc * 128 + r0 * D, [[D, 1024], [1, 128]])
                        else:
                            fc = k * 8 + j
                            in_ap = bass.AP(w2t_d, fc * 128, [[FF, D], [1, 128]])
                            out_ap = bass.AP(w2T_d, fc * 128 * D, [[D, 128], [1, D]])
                        P.wait(eng, ssem, gate)
                        P.wait(eng, sst, st_tb[b])
                        lw = P.dmat(tbb[b][:, :], in_ap, sem=sls, eng=eng)
                        P.wait(eng, sst, P.cnt[sst])
                        if wi == 0:
                            for s in range(4):
                                fcp = r0 // 256 + s
                                out_ap = bass.AP(w1T_d, fcp * 128 * 2048 + dc * 256,
                                                 [[2048, 128], [1, 256]])
                                st_tb[b] = P.dma(out_ap, tbb[b][:, s * 256:(s + 1) * 256],
                                                 sem=sst, eng=eng,
                                                 wait=(sls, lw) if s == 0 else None)
                        else:
                            st_tb[b] = P.dma(out_ap, tbb[b][:, :], sem=sst, eng=eng,
                                             wait=(sls, lw))

                cv_h_w2 = 0
                for wi in range(2):
                    ps = psA if wi == 0 else psB
                    if wi == 0:
                        src, dst, tot, rs = w1, w1t_d, FF * D, D
                        chunks = [ch * 128 * D for ch in range(32)]
                        eng, sls, sst, ssem = "sync", "LW", "TS", "ST"
                    else:
                        src, dst, tot, rs = w2, w2t_d, D * FF, FF
                        chunks = [r * 128 * FF + c * 1024
                                  for c in range(4) for r in range(8)]
                        eng, sls, sst, ssem = "gpsimd", "GX", "TS2", "LY"
                    P.wait("vector", "ST", P.cnt["ST"])
                    ct_mm = 0
                    gate0 = ("CS", P.cnt["CS"]) if wi == 0 else ("CV", P.cnt["CV"])
                    npair = len(chunks) // 2
                    cv_cp = {q: P.cnt["CV"] for q in range(4)}
                    def emit_p1_dmas(k):
                        lw = None
                        for b in range(2):
                            i = 2 * k + b
                            q = i % 4
                            g = gate0 if k == 0 else (
                                None if b == 1 else
                                ("CV", max(cv_cp[i % 4], cv_cp[(i + 1) % 4])))
                            lw = P.dma(wstg[q][:, :],
                                       bass.AP(src, chunks[i], [[rs, 128], [1, 1024]]),
                                       eng=eng, sem=sls, wait=g)
                        return lw
                    emit_p1_dmas(0)
                    for k in range(npair):
                        if k + 1 < npair:
                            emit_p1_dmas(k + 1)
                        P.wait("scalar", sls, P.cnt[sls])
                        cs_abs = None
                        for b in range(2):
                            q = (2 * k + b) % 4
                            cs_abs = P.op("scalar", lambda e, q=q: e.activation(
                                wstg[q][:, :], wstg[q][:, :], AF.Abs),
                                reads=[bx(wstg[q], 0, 128, 0, 1024)],
                                writes=[bx(wstg[q], 0, 128, 0, 1024)], inc=("CS", 1))
                        P.wait("vector", "CS", cs_abs)
                        P.wait("vector", "CT", ct_mm)
                        for b in range(2):
                            q = (2 * k + b) % 4
                            cv_cp[q] = P.op("vector", lambda e, q=q, b=b: e.tensor_copy(
                                bfb[b][:, :], wstg[q][:, :]),
                                reads=[bx(wstg[q], 0, 128, 0, 1024)],
                                writes=[bx(bfb[b], 0, 128, 0, 1024)], inc=("CV", 1))
                        P.wait("tensor", "CV", cv_cp[(2 * k + 1) % 4])
                        mml = []
                        for b in range(2):
                            for c0 in (0, 512):
                                mml.append((ps[0:1, 1536:2048], ones128[:, 0:1],
                                            bfb[b][:, c0:c0 + 512],
                                            k == 0 and b == 0 and c0 == 0,
                                            k == npair - 1 and b == 1 and c0 == 512))
                        ct_mm = P.mms(mml)
                    ct_sum = P.cnt["CT"]
                    P.wait("vector", "CT", ct_sum)
                    sc = 8 + wi
                    P.op("vector", lambda e, sc=sc, ps=ps: e.reduce_sum(
                        smalls[0:1, sc:sc + 1], ps[0:1, 1536:2048], AX.X),
                        writes=[bx(smalls, 0, 1, sc, sc + 1)])
                    cv_s = P.op("vector", lambda e, sc=sc, tot=tot: e.tensor_scalar(
                        smalls[0:1, sc:sc + 1], smalls[0:1, sc:sc + 1],
                        1.0 / tot, 1e-8, ALU.mult, ALU.max),
                        reads=[bx(smalls, 0, 1, sc, sc + 1)],
                        writes=[bx(smalls, 0, 1, sc, sc + 1)], inc=("CV", 1))
                    P.wait("tensor", "CV", cv_s)
                    ct_b = P.mms([(ps[:, 0:1], onescol[0:1, :], smalls[0:1, sc:sc + 1],
                                   True, True)])
                    P.wait("vector", "CT", ct_b)
                    P.op("vector", lambda e, ps=ps: e.tensor_scalar(
                        hbc[:, :], ps[:, 0:1], 0.5, None, ALU.mult),
                        writes=[bx(hbc, 0, 128, 0, 1)])
                    cv_h = P.op("vector", lambda e, ps=ps: e.tensor_scalar(
                        hbn[:, :], ps[:, 0:1], -0.5, None, ALU.mult),
                        writes=[bx(hbn, 0, 128, 0, 1)], inc=("CV", 1))
                    if wi == 1:
                        cv_h_w2 = cv_h
                    cv_tern = {q: cv_h for q in range(4)}
                    st_store = 0
                    def emit_p2_dmas(k):
                        for b in range(2):
                            i = 2 * k + b
                            q = i % 4
                            g = None if b == 1 else (
                                "CV", max(cv_tern[i % 4], cv_tern[(i + 1) % 4]))
                            P.dma(wstg[q][:, :],
                                  bass.AP(src, chunks[i], [[rs, 128], [1, 1024]]),
                                  eng=eng, sem=sls, wait=g)
                    emit_p2_dmas(0)
                    for k in range(npair):
                        if k + 1 < npair:
                            emit_p2_dmas(k + 1)
                        P.wait("vector", sls, P.cnt[sls])
                        P.wait("vector", "CV", cv_h)
                        P.wait("vector", ssem, st_store)
                        for b in range(2):
                            q = (2 * k + b) % 4
                            P.op("vector", lambda e, q=q: e.tensor_scalar(
                                geb[:, :], wstg[q][:, :], hbc[:, 0:1], None, ALU.is_ge),
                                reads=[bx(wstg[q], 0, 128, 0, 1024), bx(hbc, 0, 128, 0, 1)],
                                writes=[bx(geb, 0, 128, 0, 1024)])
                            P.op("vector", lambda e, q=q, b=b: e.tensor_scalar(
                                bfb[b][:, :], wstg[q][:, :], hbn[:, 0:1], None, ALU.is_le),
                                reads=[bx(wstg[q], 0, 128, 0, 1024), bx(hbn, 0, 128, 0, 1)],
                                writes=[bx(bfb[b], 0, 128, 0, 1024)])
                            cv_tern[q] = P.op("vector", lambda e, b=b: e.tensor_tensor(
                                bfb[b][:, :], geb[:, :], bfb[b][:, :], ALU.subtract),
                                reads=[bx(geb, 0, 128, 0, 1024), bx(bfb[b], 0, 128, 0, 1024)],
                                writes=[bx(bfb[b], 0, 128, 0, 1024)], inc=("CV", 1))
                        for b in range(2):
                            q = (2 * k + b) % 4
                            st_store = P.dma(bass.AP(dst, chunks[2 * k + b],
                                                     [[rs, 128], [1, 1024]]),
                                             bfb[b][:, :], sem=ssem,
                                             eng="scalar" if wi == 0 else eng,
                                             wait=("CV", cv_tern[q]))
                        if wi == 0 and k % 4 == 3:
                            transpose_group(wi, k // 4, eng, sls, sst, ssem)
                    if wi == 0:
                        sts.append(max(st_tb[0], st_tb[1]))

                def w2rt(k):
                    transpose_group(1, k, "sync", "LZ", "TS", "LY")
            return {"w1_ts": sts[0], "cv_h_w2": cv_h_w2, "w2rt": w2rt}

        def ffn0_layer(xT_hi, xT_lo, tern_st, gates):
            TB = 256
            with contextlib.ExitStack() as S:
                w1tp = [sb("f_w1a", [128, 2048], BF16, S), sb("f_w1b", [128, 2048], BF16, S)]
                w2tp = [sb("f_w2a", [128, 1024], BF16, S), sb("f_w2b", [128, 1024], BF16, S)]
                hT_hi = sb("f_hthi", [128, 32 * TB], BF16, S)
                hT_lo = sb("f_htlo", [128, 32 * TB], BF16, S)
                hg = sb("f_hg", [128, TB], F32, S)
                b1t = sb("f_b1t", [128, 32], F32, S)

                P.wait("sync", "CV", gates["cv"])
                P.wait("sync", "LW", gates["lw"])
                P.dma(b1t[:, :], bass.AP(b1, 0, [[1, 128], [128, 32]]), slow=True)
                P.wait("sync", "CS", gates["cs"])
                bld = P.dma(btile[:, :], bass.AP(b2, 0, [[0, 128], [1, D]]), slow=True)
                P.wait("scalar", "LW", bld)
                P.wait("vector", "LW", bld)
                bt_done = None
                for tc in range(TC):
                    b = bx(x_res, 0, 128, tc * D, (tc + 1) * D)
                    bt_done = P.op("vector", lambda e, tc=tc: e.tensor_tensor(
                        x_res[:, tc * D:(tc + 1) * D], x_res[:, tc * D:(tc + 1) * D],
                        btile[:, :], ALU.add),
                        reads=[b, bx(btile, 0, 128, 0, D)], writes=[b], inc=("CV", 1))

                lw1 = {0: 0, 1: 0}
                use1 = {0: 0, 1: 0}
                lw2 = {0: 0, 1: 0}
                use2 = {0: 0, 1: 0}
                cs_g_reg = [0] * 4
                cv_h_prev = 0
                w2_ts_ready = 0
                for tb in range(8):
                    for fcp in range(16):
                        if tb == 0 and fcp % 4 == 3:
                            tern_st["w2rt"](fcp // 4)
                            w2_ts_ready = P.cnt["TS"]
                        b = fcp % 2
                        in_ap = bass.AP(w1T_d, fcp * 128 * 2048, [[2048, 128], [1, 2048]])
                        out_ap = bass.AP(w1tp[b], 0, [[2048, 128], [1, 2048]])
                        if tb == 0 and fcp < 2:
                            P.wait("sync", "TS", tern_st["w1_ts"])
                        lw1[b] = P.dma(out_ap, in_ap, sem="LA%d" % b,
                                       wait=("CT", use1[b]))
                        P.wait("tensor", "LA%d" % b, lw1[b])
                        for sub in range(2):
                            fc = fcp * 2 + sub
                            reg0 = (fc % 4) * 512
                            P.wait("tensor", "CS", cs_g_reg[fc % 4])
                            mml = []
                            for dc in range(DC):
                                for pi, part in enumerate((xT_hi, xT_lo)):
                                    mml.append((psA[:, reg0:reg0 + TB],
                                                w1tp[b][:, dc * 256 + sub * 128:
                                                        dc * 256 + sub * 128 + 128],
                                                part[:, dc * T + tb * TB: dc * T + tb * TB + TB],
                                                dc == 0 and pi == 0,
                                                dc == DC - 1 and pi == 1))
                            ct_h = P.mms(mml)
                            use1[b] = ct_h
                            P.wait("scalar", "CT", ct_h)
                            P.wait("scalar", "CV", cv_h_prev)
                            cs_g = P.op("scalar", lambda e, fc=fc, reg0=reg0: e.activation(
                                hg[:, :], psA[:, reg0:reg0 + TB], AF.Gelu,
                                bias=b1t[:, fc:fc + 1], scale=1.0),
                                writes=[bx(hg, 0, 128, 0, TB)], inc=("CS", 1))
                            cs_g_reg[fc % 4] = cs_g
                            P.wait("vector", "CS", cs_g)
                            P.op("vector", lambda e, fc=fc: e.tensor_copy(
                                hT_hi[:, fc * TB:(fc + 1) * TB], hg[:, :]),
                                reads=[bx(hg, 0, 128, 0, TB)],
                                writes=[bx(hT_hi, 0, 128, fc * TB, (fc + 1) * TB)])
                            cv_h_prev = P.op("vector", lambda e, fc=fc: e.tensor_tensor(
                                hT_lo[:, fc * TB:(fc + 1) * TB], hg[:, :],
                                hT_hi[:, fc * TB:(fc + 1) * TB], ALU.subtract),
                                reads=[bx(hg, 0, 128, 0, TB),
                                       bx(hT_hi, 0, 128, fc * TB, (fc + 1) * TB)],
                                writes=[bx(hT_lo, 0, 128, fc * TB, (fc + 1) * TB)],
                                inc=("CV", 1))
                    cv_h_done = cv_h_prev
                    P.wait("tensor", "CV", cv_h_done)
                    ct_f2 = None
                    for fc in range(32):
                        b = fc % 2
                        if tb == 0 and fc < 2:
                            P.wait("sync", "TS", w2_ts_ready)
                        if tb == 0 and fc == 0:
                            P.wait("tensor", "CV", tern_st["cv_h_w2"])
                        lw2[b] = P.dma(w2tp[b][:, :],
                                       bass.AP(w2T_d, fc * 128 * D, [[D, 128], [1, D]]),
                                       sem="LB%d" % b, wait=("CT", use2[b]))
                        P.wait("tensor", "LB%d" % b, lw2[b])
                        mml = []
                        for tcl in range(2):
                            reg0 = tcl * 1024
                            for c0 in range(0, 1024, 512):
                                for pi, hh in enumerate((hT_hi, hT_lo)):
                                    mml.append((psB[:, reg0 + c0:reg0 + c0 + 512],
                                                hh[:, fc * TB + tcl * 128: fc * TB + tcl * 128 + 128],
                                                w2tp[b][:, c0:c0 + 512],
                                                fc == 0 and pi == 0,
                                                fc == 31 and pi == 1))
                        ct_f2 = P.mms(mml)
                        use2[b] = ct_f2
                    P.wait("vector", "CT", ct_f2)
                    for tcl in range(2):
                        tc = tb * 2 + tcl
                        bxx = bx(x_res, 0, 128, tc * D, (tc + 1) * D)
                        P.op("vector", lambda e, tc=tc, tcl=tcl: e.tensor_tensor(
                            x_res[:, tc * D:(tc + 1) * D], x_res[:, tc * D:(tc + 1) * D],
                            psB[:, tcl * 1024:tcl * 1024 + 1024], ALU.add),
                            reads=[bxx], writes=[bxx], inc=("CV", 1))
                return ln_inplace(P.cnt["CV"])

        def ffn1_layer():
            with contextlib.ExitStack() as S:
                W = 2048
                kq = sb("q_kq", [128, DC * 1024], BF16, S)
                kstg = sb("q_kstg", [128, 1024], BF16, S)
                k8 = sb("q_k8", [128, 1024], FP8, S)
                kint = sb("q_kint", [128, 1024], I8, S)
                g0 = sb("q_g0", [128, W], F32, S)
                g1 = sb("q_g1", [128, W], F32, S)
                g2 = sb("q_g2", [128, W], F32, S)
                g3 = sb("q_g3", [128, W], F32, S)
                g4 = sb("q_g4", [128, W], F32, S)
                g5 = sb("q_g5", [128, W], F32, S)
                gax = sb("q_gax", [128, W], F32, S)
                gsg = sb("q_gsg", [128, W], F32, S)
                hwb = sb("q_hwb", [128, W], BF16, S)
                hw8 = sb("q_hw8", [128, W], FP8, S)
                hww = sb("q_hww", [128, W], F32, S)
                xa = sb("q_xa", [128, W], F32, S)
                xqT = sb("q_xqT", [128, W], BF16, S)
                yq = sb("q_yq", [128, W], F32, S)
                w01 = sb("q_w01", [128, 1], F32, S)
                ssbc = sb("q_ssbc", [128, 1], F32, S)
                rsibc = sb("q_rsibc", [128, 1], F32, S)
                bl2 = sb("q_bl2", [128, W], BF16, S)

                def vop(fn, reads, writes, inc=None):
                    return P.op("vector", fn, reads=reads, writes=writes, inc=inc)

                def B(t, f0=0, f1=None):
                    return bx(t, 0, 128, f0, W if f1 is None else f1)

                P.wait("sync", "CV", P.cnt["CV"])
                P.wait("sync", "LW", P.cnt["LW"])
                P.dma(smalls[0:1, 40:41], bass.AP(sin1, 0, [[1, 1], [1, 1]]), slow=True)
                P.dma(smalls[0:1, 41:42], bass.AP(sk1, 0, [[1, 1], [1, 1]]), slow=True)
                P.dma(bl2[:, 0:1024], bass.AP(bias1, 0, [[0, 128], [1, D]]), slow=True)
                bld = P.dma(bl2[:, 1024:2048], bass.AP(bias1, 0, [[0, 128], [1, D]]), slow=True)
                P.wait("vector", "LW", bld)
                P.op("vector", lambda e: e.tensor_tensor(
                    smalls[0:1, 42:43], smalls[0:1, 40:41], smalls[0:1, 41:42], ALU.mult),
                    reads=[bx(smalls, 0, 1, 40, 42)], writes=[bx(smalls, 0, 1, 42, 43)])
                P.op("vector", lambda e: e.reciprocal(smalls[0:1, 43:44], smalls[0:1, 41:42]),
                     reads=[bx(smalls, 0, 1, 41, 42)], writes=[bx(smalls, 0, 1, 43, 44)])
                cv_sc = P.op("vector", lambda e: e.reciprocal(smalls[0:1, 44:45], smalls[0:1, 40:41]),
                             reads=[bx(smalls, 0, 1, 40, 41)], writes=[bx(smalls, 0, 1, 44, 45)],
                             inc=("CV", 1))
                P.wait("tensor", "CV", cv_sc)
                ct_bc = P.mms([(psA[:, 0:1], onescol[0:1, :], smalls[0:1, 42:43], True, True),
                               (psA[:, 1:2], onescol[0:1, :], smalls[0:1, 44:45], True, True),
                               (psA[:, 2:3], onescol[0:1, :], smalls[0:1, 43:44], True, True)])
                P.wait("vector", "CT", ct_bc)
                P.op("vector", lambda e: e.tensor_copy(ssbc[:, :], psA[:, 0:1]),
                     writes=[bx(ssbc, 0, 128, 0, 1)])
                P.op("vector", lambda e: e.tensor_copy(rsibc[:, :], psA[:, 1:2]),
                     writes=[bx(rsibc, 0, 128, 0, 1)])
                cv_bc2 = P.op("vector", lambda e: e.tensor_copy(smalls[:, 45:46], psA[:, 2:3]),
                              writes=[bx(smalls, 0, 128, 45, 46)], inc=("CV", 1))

                P.op("vector", lambda e: e.memset(w01[:, :], 2.0 ** -30),
                     writes=[bx(w01, 0, 128, 0, 1)])
                P.op("vector", lambda e: e.tensor_scalar(
                    w01[:, :], w01[:, :], 1.0, 1.0, ALU.add, ALU.subtract),
                    reads=[bx(w01, 0, 128, 0, 1)], writes=[bx(w01, 0, 128, 0, 1)])
                P.op("vector", lambda e: e.tensor_scalar(
                    w01[:, :], w01[:, :], 2.0 ** -31, None, ALU.is_ge),
                    reads=[bx(w01, 0, 128, 0, 1)], writes=[bx(w01, 0, 128, 0, 1)])

                kq_done = None
                for dc in range(DC):
                    lw = P.dma(kstg[:, :], bass.AP(kern1, dc * 128 * D, [[D, 128], [1, D]]),
                               wait=("CV", P.cnt["CV"]))
                    P.wait("vector", "LW", lw)
                    P.op("vector", lambda e: e.tensor_scalar(
                        g0[:, 0:1024], kstg[:, :], smalls[:, 45:46], None, ALU.mult),
                        reads=[bx(kstg, 0, 128, 0, 1024), bx(smalls, 0, 128, 45, 46)],
                        writes=[bx(g0, 0, 128, 0, 1024)])
                    P.op("vector", lambda e: e.tensor_copy(k8[:, :], g0[:, 0:1024]),
                         reads=[bx(g0, 0, 128, 0, 1024)], writes=[bx(k8, 0, 128, 0, 1024)])
                    P.op("vector", lambda e: e.tensor_copy(kint[:, :], k8[:, :].bitcast(I8)),
                         reads=[bx(k8, 0, 128, 0, 1024)], writes=[bx(kint, 0, 128, 0, 1024)])
                    kq_done = P.op("vector", lambda e, dc=dc: e.tensor_copy(
                        kq[:, dc * 1024:(dc + 1) * 1024], kint[:, :].bitcast(FP8)),
                        reads=[bx(kint, 0, 128, 0, 1024)],
                        writes=[bx(kq, 0, 128, dc * 1024, (dc + 1) * 1024)], inc=("CV", 1))

                def exp_chain(clamp_lo, bias_exp):
                    vop(lambda e: e.tensor_scalar(
                        g1[:, :], g0[:, :], INVLN2, 511.5, ALU.mult, ALU.add),
                        [B(g0)], [B(g1)])
                    vop(lambda e: e.tensor_scalar(
                        g1[:, :], g1[:, :], B3, B3, ALU.add, ALU.subtract),
                        [B(g1)], [B(g1)])
                    vop(lambda e, c=float(clamp_lo): e.tensor_scalar(
                        g1[:, :], g1[:, :], c, 560.0, ALU.max, ALU.min),
                        [B(g1)], [B(g1)])
                    vop(lambda e, b=float(bias_exp): e.tensor_scalar(
                        g1[:, :], g1[:, :], -1.0, b, ALU.mult, ALU.add),
                        [B(g1)], [B(g1)])
                    vop(lambda e: e.tensor_scalar(
                        g2[:, :], g1[:, :], LH, None, ALU.mult),
                        [B(g1)], [B(g2)])
                    return vop(lambda e: e.scalar_tensor_tensor(
                        g1[:, :], g1[:, :], LL, g2[:, :], ALU.mult, ALU.add),
                        [B(g1), B(g2)], [B(g1)], inc=("CV", 1))

                def abs_sign_ln(src_ap, src_bx):
                    P.op("scalar", lambda e: e.activation(gax[:, :], src_ap, AF.Abs),
                         reads=[src_bx], writes=[B(gax)])
                    P.op("scalar", lambda e: e.activation(gsg[:, :], src_ap, AF.Sign),
                         reads=[src_bx], writes=[B(gsg)])
                    return P.op("scalar", lambda e: e.activation(
                        g0[:, :], gax[:, :], AF.Ln, bias=cb_tiny[:, 0:1], scale=1.0),
                        reads=[B(gax)], writes=[B(g0)], inc=("CS", 1))

                def exp_pair():
                    P.op("scalar", lambda e: e.activation(
                        g2[:, :], g1[:, :], AF.Exp, bias=0.0, scale=1.0),
                        reads=[B(g1)], writes=[B(g2)])
                    return P.op("scalar", lambda e: e.activation(
                        g3[:, :], g1[:, :], AF.Exp, bias=0.0, scale=-1.0),
                        reads=[B(g1)], writes=[B(g3)], inc=("CS", 1))

                P.wait("vector", "LW", bld)
                cv_xqT_prev = P.cnt["CV"]
                cs_y_prev = P.cnt["CS"]
                cv_res_prev = P.cnt["CV"]
                for p in range(8):
                    X = x_res[:, p * W:(p + 1) * W]
                    Xb = bx(x_res, 0, 128, p * W, (p + 1) * W)
                    P.wait("scalar", "CV", cv_res_prev)
                    cs1 = abs_sign_ln(X, Xb)
                    P.wait("vector", "CS", cs1)
                    cv1 = exp_chain(440, 535)
                    P.wait("scalar", "CV", cv1)
                    cs2 = exp_pair()
                    P.wait("vector", "CS", cs2)
                    vop(lambda e: e.tensor_tensor(g0[:, :], gax[:, :], g2[:, :], ALU.mult),
                        [B(gax), B(g2)], [B(g0)])
                    vop(lambda e: e.tensor_scalar(
                        g1[:, :], g0[:, :], B3, B3, ALU.add, ALU.subtract),
                        [B(g0)], [B(g1)])
                    vop(lambda e: e.tensor_scalar(
                        g4[:, :], g1[:, :], 2.0 ** -16, None, ALU.mult),
                        [B(g1)], [B(g4)])
                    vop(lambda e: e.tensor_scalar(
                        g4[:, :], g4[:, :], B3, B3, ALU.add, ALU.subtract),
                        [B(g4)], [B(g4)])
                    vop(lambda e: e.tensor_tensor(g4[:, :], g4[:, :], g3[:, :], ALU.mult),
                        [B(g4), B(g3)], [B(g4)])
                    vop(lambda e: e.tensor_tensor(g4[:, :], g4[:, :], gsg[:, :], ALU.mult),
                        [B(g4), B(gsg)], [B(g4)])
                    vop(lambda e: e.tensor_scalar(
                        g4[:, :], g4[:, :], 65536.0, None, ALU.mult),
                        [B(g4)], [B(g4)])
                    vop(lambda e, X=X: e.tensor_copy(hwb[:, :], X), [Xb], [B(hwb)])
                    vop(lambda e: e.tensor_copy(hww[:, :], hwb[:, :]), [B(hwb)], [B(hww)])
                    vop(lambda e: e.tensor_tensor(g5[:, :], g4[:, :], hww[:, :], ALU.subtract),
                        [B(g4), B(hww)], [B(g5)])
                    vop(lambda e: e.tensor_scalar(
                        g5[:, :], g5[:, :], w01[:, 0:1], None, ALU.mult),
                        [B(g5), bx(w01, 0, 128, 0, 1)], [B(g5)])
                    vop(lambda e: e.tensor_tensor(xa[:, :], g5[:, :], hww[:, :], ALU.add),
                        [B(g5), B(hww)], [B(xa)])
                    cv2 = vop(lambda e: e.tensor_scalar(
                        g5[:, :], xa[:, :], rsibc[:, 0:1], None, ALU.mult),
                        [B(xa), bx(rsibc, 0, 128, 0, 1)], [B(g5)], inc=("CV", 1))
                    P.wait("scalar", "CV", cv2)
                    cs3 = abs_sign_ln(g5[:, :], B(g5))
                    P.wait("vector", "CS", cs3)
                    cv3 = exp_chain(506, 515)
                    P.wait("scalar", "CV", cv3)
                    cs4 = exp_pair()
                    P.wait("vector", "CS", cs4)
                    vop(lambda e: e.tensor_copy(hw8[:, :], g5[:, :]), [B(g5)], [B(hw8)])
                    vop(lambda e: e.tensor_copy(hww[:, :], hw8[:, :]), [B(hw8)], [B(hww)])
                    vop(lambda e: e.tensor_tensor(g0[:, :], gax[:, :], g2[:, :], ALU.mult),
                        [B(gax), B(g2)], [B(g0)])
                    vop(lambda e: e.tensor_scalar(
                        g0[:, :], g0[:, :], B3S, B3S, ALU.add, ALU.subtract),
                        [B(g0)], [B(g0)])
                    vop(lambda e: e.tensor_scalar(
                        g1[:, :], g0[:, :], B3, B3, ALU.add, ALU.subtract),
                        [B(g0)], [B(g1)])
                    vop(lambda e: e.tensor_tensor(g0[:, :], g1[:, :], g3[:, :], ALU.mult),
                        [B(g1), B(g3)], [B(g0)])
                    vop(lambda e: e.tensor_tensor(g0[:, :], g0[:, :], gsg[:, :], ALU.mult),
                        [B(g0), B(gsg)], [B(g0)])
                    vop(lambda e: e.tensor_tensor(g1[:, :], g0[:, :], hww[:, :], ALU.subtract),
                        [B(g0), B(hww)], [B(g1)])
                    vop(lambda e: e.tensor_scalar(
                        g1[:, :], g1[:, :], w01[:, 0:1], None, ALU.mult),
                        [B(g1), bx(w01, 0, 128, 0, 1)], [B(g1)])
                    cv_yq = vop(lambda e: e.tensor_tensor(
                        yq[:, :], g1[:, :], hww[:, :], ALU.add),
                        [B(g1), B(hww)], [B(yq)], inc=("CV", 1))
                    P.wait("tensor", "CV", cv_yq)
                    P.wait("tensor", "CV", cv_xqT_prev)
                    P.wait("tensor", "CS", cs_y_prev)
                    tl = [(psA[:, k * 128:(k + 1) * 128],
                           yq[:, k * 128:(k + 1) * 128], idf[:, :]) for k in range(16)]
                    ct_tr = P.trans(tl)
                    P.wait("vector", "CT", ct_tr)
                    cv_xqT = P.op("vector", lambda e: e.tensor_copy(xqT[:, :], psA[:, :]),
                                  writes=[B(xqT)], inc=("CV", 1))
                    cv_xqT_prev = cv_xqT
                    P.wait("tensor", "CV", cv_xqT)
                    P.wait("tensor", "CV", kq_done)
                    mml = []
                    for tcl in range(2):
                        for dc in range(DC):
                            for c0 in (0, 512):
                                mml.append((psB[:, tcl * 1024 + c0:tcl * 1024 + c0 + 512],
                                            xqT[:, tcl * 1024 + dc * 128:tcl * 1024 + dc * 128 + 128],
                                            kq[:, dc * 1024 + c0:dc * 1024 + c0 + 512],
                                            dc == 0, dc == DC - 1))
                    ct_mm = P.mms(mml)
                    P.wait("scalar", "CT", ct_mm)
                    cs_y = P.op("scalar", lambda e: e.activation(
                        yq[:, :], psB[:, :], AF.Identity, bias=0.0, scale=ssbc[:, 0:1]),
                        reads=[bx(ssbc, 0, 128, 0, 1)], writes=[B(yq)], inc=("CS", 1))
                    cs_y_prev = cs_y
                    cs5 = abs_sign_ln(yq[:, :], B(yq))
                    P.wait("vector", "CS", cs5)
                    vop(lambda e: e.tensor_copy(hwb[:, :], yq[:, :]), [B(yq)], [B(hwb)])
                    vop(lambda e: e.tensor_copy(hww[:, :], hwb[:, :]), [B(hwb)], [B(hww)])
                    cv4 = exp_chain(440, 519)
                    P.wait("scalar", "CV", cv4)
                    cs6 = exp_pair()
                    P.wait("vector", "CS", cs6)
                    vop(lambda e: e.tensor_tensor(g0[:, :], gax[:, :], g2[:, :], ALU.mult),
                        [B(gax), B(g2)], [B(g0)])
                    vop(lambda e: e.tensor_scalar(
                        g1[:, :], g0[:, :], B3, B3, ALU.add, ALU.subtract),
                        [B(g0)], [B(g1)])
                    vop(lambda e: e.tensor_scalar(
                        g2[:, :], g0[:, :], B5, B5, ALU.add, ALU.subtract),
                        [B(g0)], [B(g2)])
                    vop(lambda e: e.tensor_tensor(g4[:, :], g0[:, :], g1[:, :], ALU.subtract),
                        [B(g0), B(g1)], [B(g4)])
                    vop(lambda e: e.tensor_tensor(g4[:, :], g4[:, :], g4[:, :], ALU.mult),
                        [B(g4)], [B(g4)])
                    vop(lambda e: e.tensor_scalar(
                        g4[:, :], g4[:, :], TIE_THR, None, ALU.is_ge),
                        [B(g4)], [B(g4)])
                    vop(lambda e: e.tensor_tensor(g2[:, :], g2[:, :], g1[:, :], ALU.subtract),
                        [B(g2), B(g1)], [B(g2)])
                    vop(lambda e: e.tensor_tensor(g2[:, :], g2[:, :], g4[:, :], ALU.mult),
                        [B(g2), B(g4)], [B(g2)])
                    vop(lambda e: e.tensor_tensor(g1[:, :], g1[:, :], g2[:, :], ALU.add),
                        [B(g1), B(g2)], [B(g1)])
                    vop(lambda e: e.tensor_tensor(g0[:, :], g1[:, :], g3[:, :], ALU.mult),
                        [B(g1), B(g3)], [B(g0)])
                    vop(lambda e: e.tensor_tensor(g0[:, :], g0[:, :], gsg[:, :], ALU.mult),
                        [B(g0), B(gsg)], [B(g0)])
                    vop(lambda e: e.tensor_tensor(g1[:, :], g0[:, :], hww[:, :], ALU.subtract),
                        [B(g0), B(hww)], [B(g1)])
                    vop(lambda e: e.tensor_scalar(
                        g1[:, :], g1[:, :], w01[:, 0:1], None, ALU.mult),
                        [B(g1), bx(w01, 0, 128, 0, 1)], [B(g1)])
                    vop(lambda e: e.tensor_tensor(g1[:, :], g1[:, :], hww[:, :], ALU.add),
                        [B(g1), B(hww)], [B(g1)])
                    vop(lambda e: e.tensor_tensor(g1[:, :], g1[:, :], bl2[:, :], ALU.add),
                        [B(g1), B(bl2)], [B(g1)])
                    cv_res_prev = vop(lambda e, X=X: e.tensor_tensor(X, X, g1[:, :], ALU.add),
                                      [Xb, B(g1)], [Xb], inc=("CV", 1))
                return ln_inplace(P.cnt["CV"])

        last_cv = 0
        with contextlib.ExitStack() as SL:
            xTf = SL.enter_context(nc.sbuf_tensor("xTf0", [128, DC * T], F32))
            build_xT_f32(xTf, 0)
            last_cv = attn_layer(0, xTf, 0)
        if parts >= 2:
            with contextlib.ExitStack() as SL:
                xT_hi = SL.enter_context(nc.sbuf_tensor("xT_hi", [128, DC * T], BF16))
                xT_lo = SL.enter_context(nc.sbuf_tensor("xT_lo", [128, DC * T], BF16))
                build_xT(xT_hi, xT_lo, last_cv, alt=False)
                gates = {"cv": P.cnt["CV"], "cs": P.cnt["CS"], "lw": P.cnt["LW"]}
                with contextlib.ExitStack() as TSK:
                    tern_st = ternarize_weights(TSK)
                    last_cv = ffn0_layer(xT_hi, xT_lo, tern_st, gates)
        if parts >= 3:
            with contextlib.ExitStack() as SL:
                xTf = SL.enter_context(nc.sbuf_tensor("xTf1", [128, DC * T], F32))
                build_xT_f32(xTf, last_cv)
                last_cv = attn_layer(1, xTf, last_cv)
        if parts >= 4:
            last_cv = ffn1_layer()

        for tc in range(TC):
            P.dma(bass.AP(out_t, tc * 128 * D, [[D, 128], [1, D]]),
                  x_res[:, tc * D:(tc + 1) * D], sem="ST", wait=("CV", last_cv))
        P.wait("sync", "ST", P.cnt["ST"])

        with nc.Block() as block:
            @block.sync
            def _(e):
                for f in P.ops["sync"]:
                    f(e)

            @block.vector
            def _(e):
                for f in P.ops["vector"]:
                    f(e)

            @block.scalar
            def _(e):
                for f in P.ops["scalar"]:
                    f(e)

            @block.tensor
            def _(e):
                for f in P.ops["tensor"]:
                    f(e)

            @block.gpsimd
            def _(e):
                for f in P.ops["gpsimd"]:
                    f(e)

    return nc


def _consts():
    r = np.arange(128)
    diag = np.where(r[:, None] > r[None, :], np.float32(-8e9), np.float32(0.0)).astype(np.float32)
    return {
        "ident_f": np.eye(128, dtype=np.float32),
        "ident_b": np.eye(128, dtype=np.float32).astype(ml_dtypes.bfloat16),
        "diagm": diag,
    }


def kernel(**inputs):
    nc = build_program()
    consts = _consts()
    weights = {k: np.ascontiguousarray(v) for k, v in inputs.items()
               if k != "L0_attn_embeddings"}
    x = np.ascontiguousarray(inputs["L0_attn_embeddings"])
    in_maps = [dict(weights, **consts, x_in=np.ascontiguousarray(x[i])) for i in range(8)]
    res = run_bass_kernel_spmd(nc, in_maps, core_ids=list(range(8)))
    return np.stack([np.asarray(r["out"]) for r in res.results], axis=0)


# version 25
# speedup vs baseline: 1.0077x; 1.0077x over previous
import contextlib
import math

import numpy as np
import ml_dtypes

import concourse.bass as bass
import concourse.mybir as mybir
from concourse.bass_utils import run_bass_kernel_spmd

F32 = mybir.dt.float32
BF16 = mybir.dt.bfloat16
FP8 = mybir.dt.float8e4
I8 = mybir.dt.int8
AF = mybir.ActivationFunctionType
ALU = mybir.AluOpType
AX = mybir.AxisListType

T = 2048
TC = 16
D = 1024
DC = 8
H = 16
DK = 64
FF = 4096

B3 = float(np.float32(1.5 * 2.0 ** 52))
B3S = float(np.float32(1.5 * 2.0 ** 43))
B3T = float(np.float32(1.5 * 2.0 ** 36))
B5 = float(np.float32(1.5 * 2.0 ** 53))
LH = float(np.float32(math.log(2.0)))
LL = float(np.float32(math.log(2.0) - np.float64(np.float32(math.log(2.0)))))
INVLN2 = 1.0 / math.log(2.0)
TIE_THR = float(np.float32((0.5 - 2.0 ** -20) ** 2))

ENGINES = ("sync", "vector", "scalar", "tensor", "gpsimd")


def bx(t, p0, p1, f0, f1):
    return (t, p0, p1, f0, f1)


class Prog:

    def __init__(self, nc):
        self.nc = nc
        self.ops = {e: [] for e in ENGINES}
        self.cnt = {}
        self.semh = {}
        self.dw = {e: [] for e in ENGINES}
        self.dr = {e: [] for e in ENGINES}

    def sem(self, es, name):
        self.semh[name] = es.enter_context(self.nc.semaphore(name))
        self.cnt[name] = 0

    @staticmethod
    def _ov(a, b):
        return (a[0] is b[0]) and a[1] < b[2] and b[1] < a[2] and a[3] < b[4] and b[3] < a[4]

    def wait(self, e, sname, val):
        h = self.semh[sname]
        self.ops[e].append(lambda eng: eng.wait_ge(h, val))

    def drain(self, e):
        self.ops[e].append(lambda eng: eng.drain())
        self.dw[e].clear()
        self.dr[e].clear()

    def op(self, e, fn, reads=(), writes=(), inc=None, drainable=True):
        if drainable:
            hit = any(any(self._ov(r, w) for w in self.dw[e]) for r in reads)
            if not hit:
                hit = any(any(self._ov(w, x) for x in self.dw[e]) or
                          any(self._ov(w, x) for x in self.dr[e]) for w in writes)
            if hit:
                self.drain(e)
        if inc is not None:
            s, n = inc
            h = self.semh[s]
            self.ops[e].append(lambda eng: fn(eng).then_inc(h, n))
            self.cnt[s] += n
        else:
            self.ops[e].append(lambda eng: fn(eng))
        self.dr[e].extend(reads)
        self.dw[e].extend(writes)
        return self.cnt.get(inc[0]) if inc else None

    def dma(self, out_ap, in_ap, sem="LW", eng="sync", wait=None, slow=False):
        if wait is not None:
            self.wait(eng, wait[0], wait[1])
        h = self.semh[sem]
        if slow:
            self.ops[eng].append(
                lambda e: e.dma_start(out=out_ap, in_=in_ap,
                                      allow_slow_non_contiguous=True).then_inc(h, 16))
        else:
            self.ops[eng].append(lambda e: e.dma_start(out=out_ap, in_=in_ap).then_inc(h, 16))
        self.cnt[sem] += 16
        return self.cnt[sem]

    def dmat(self, out_ap, in_ap, sem="LW", eng="sync", wait=None):
        if wait is not None:
            self.wait(eng, wait[0], wait[1])
        h = self.semh[sem]
        self.ops[eng].append(lambda e: e.dma_start_transpose(out_ap, in_ap).then_inc(h, 16))
        self.cnt[sem] += 16
        return self.cnt[sem]

    def mms(self, mmlist, sem="CT"):
        h = self.semh[sem]
        n = len(mmlist)
        for i, (o, l, r, st, sp) in enumerate(mmlist):
            if i == n - 1:
                self.ops["tensor"].append(
                    lambda e, o=o, l=l, r=r, st=st, sp=sp: e.matmul(
                        o, l, r, start=st, stop=sp).then_inc(h, 1))
            else:
                self.ops["tensor"].append(
                    lambda e, o=o, l=l, r=r, st=st, sp=sp: e.matmul(o, l, r, start=st, stop=sp))
        self.cnt[sem] += 1
        return self.cnt[sem]

    def trans(self, tlist, sem="CT"):
        h = self.semh[sem]
        n = len(tlist)
        for i, (o, a, idn) in enumerate(tlist):
            if i == n - 1:
                self.ops["tensor"].append(
                    lambda e, o=o, a=a, idn=idn: e.transpose(o, a, idn).then_inc(h, 1))
            else:
                self.ops["tensor"].append(lambda e, o=o, a=a, idn=idn: e.transpose(o, a, idn))
        self.cnt[sem] += 1
        return self.cnt[sem]


def build_program(parts=4):
    nc = bass.Bass("TRN2", target_bir_lowering=False)

    x_in = nc.dram_tensor("x_in", [T, D], F32, kind="ExternalInput")
    out_t = nc.dram_tensor("out", [T, D], F32, kind="ExternalOutput")
    ident_f = nc.dram_tensor("ident_f", [128, 128], F32, kind="ExternalInput")
    ident_b = nc.dram_tensor("ident_b", [128, 128], BF16, kind="ExternalInput")
    diagm = nc.dram_tensor("diagm", [128, 128], F32, kind="ExternalInput")

    wq0 = nc.dram_tensor("L0_attn_Wq", [H, D, DK], F32, kind="ExternalInput")
    bq0 = nc.dram_tensor("L0_attn_bq", [H, DK], F32, kind="ExternalInput")
    wk0 = nc.dram_tensor("L0_attn_Wk", [H, D, DK], F32, kind="ExternalInput")
    bk0 = nc.dram_tensor("L0_attn_bk", [H, DK], F32, kind="ExternalInput")
    wv0 = nc.dram_tensor("L0_attn_Wv", [H, D, DK], F32, kind="ExternalInput")
    bv0 = nc.dram_tensor("L0_attn_bv", [H, DK], F32, kind="ExternalInput")
    wo0 = nc.dram_tensor("L0_attn_Wo", [D, D], F32, kind="ExternalInput")
    bo0 = nc.dram_tensor("L0_attn_bo", [D], F32, kind="ExternalInput")
    w1 = nc.dram_tensor("L0_ffn_W1", [FF, D], F32, kind="ExternalInput")
    b1 = nc.dram_tensor("L0_ffn_b1", [FF], F32, kind="ExternalInput")
    w2 = nc.dram_tensor("L0_ffn_W2", [D, FF], F32, kind="ExternalInput")
    b2 = nc.dram_tensor("L0_ffn_b2", [D], F32, kind="ExternalInput")
    wqkv1 = nc.dram_tensor("L1_attn_W_qkv", [D, 3 * D], F32, kind="ExternalInput")
    bqkv1 = nc.dram_tensor("L1_attn_b_qkv", [3 * D], F32, kind="ExternalInput")
    wout1 = nc.dram_tensor("L1_attn_W_out", [D, D], F32, kind="ExternalInput")
    bout1 = nc.dram_tensor("L1_attn_b_out", [D], F32, kind="ExternalInput")
    kern1 = nc.dram_tensor("L1_ffn_kernel", [D, D], BF16, kind="ExternalInput")
    sin1 = nc.dram_tensor("L1_ffn_input_scale", [1], F32, kind="ExternalInput")
    sk1 = nc.dram_tensor("L1_ffn_kernel_scale", [1], F32, kind="ExternalInput")
    bias1 = nc.dram_tensor("L1_ffn_bias", [D], BF16, kind="ExternalInput")

    w1t_d = nc.dram_tensor("w1t_d", [FF, D], BF16)
    w1T_d = nc.dram_tensor("w1T_d", [16, 128, 2048], BF16)
    w2t_d = nc.dram_tensor("w2t_d", [D, FF], BF16)
    w2T_d = nc.dram_tensor("w2T_d", [FF, D], BF16)

    P = Prog(nc)

    with contextlib.ExitStack() as g:
        def sb(nm, shape, dt, es=g):
            return es.enter_context(nc.sbuf_tensor(nm, shape, dt))

        x_res = sb("x_res", [128, TC * D], F32)
        idf = sb("idf", [128, 128], F32)
        idb = sb("idb", [128, 128], BF16)
        dmask = sb("dmask", [128, 128], F32)
        onescol = sb("onescol", [1, 128], F32)
        ones128 = sb("ones128", [128, 1], BF16)
        smalls = sb("smalls", [128, 64], F32)
        st_sx = sb("st_sx", [128, TC], F32)
        st_sq = sb("st_sq", [128, TC], F32)
        st_m = sb("st_m", [128, TC], F32)
        st_w = sb("st_w", [128, TC], F32)
        btile = sb("btile", [128, D], F32)
        jnk = btile
        cb_eps = sb("cb_eps", [128, 1], F32)
        cb_tiny = sb("cb_tiny", [128, 1], F32)

        for s in ("LW", "LX", "GX", "LZ", "LS", "ST", "TS", "TS2", "LY", "CT", "CV", "CS",
                  "LA0", "LA1", "LB0", "LB1"):
            P.sem(g, s)

        psA = g.enter_context(nc.psum_tensor("psA", [128, 2048], F32))
        psB = g.enter_context(nc.psum_tensor("psB", [128, 2048], F32))

        P.dma(idf[:, :], ident_f.ap(), sem="LX")
        P.dma(idb[:, :], ident_b.ap(), sem="LX")
        P.dma(dmask[:, :], diagm.ap(), sem="LX")
        P.dma(bass.AP(x_res, 0, [[TC * D, 128], [D, TC], [1, D]]),
              bass.AP(x_in, 0, [[D, 128], [128 * D, TC], [1, D]]), sem="LX")
        x_loaded = P.cnt["LX"]
        P.op("vector", lambda e: e.memset(onescol[:, :], 1.0), writes=[bx(onescol, 0, 1, 0, 128)])
        P.op("vector", lambda e: e.memset(ones128[:, :], 1.0), writes=[bx(ones128, 0, 128, 0, 1)])
        P.op("vector", lambda e: e.memset(cb_eps[:, :], 1e-5), writes=[bx(cb_eps, 0, 128, 0, 1)])
        P.op("vector", lambda e: e.memset(cb_tiny[:, :], 1e-37), writes=[bx(cb_tiny, 0, 128, 0, 1)])
        P.op("vector", lambda e: e.memset(smalls[:, :], 0.0), writes=[bx(smalls, 0, 128, 0, 64)])
        P.wait("vector", "LX", x_loaded)
        P.wait("scalar", "LX", x_loaded)
        P.wait("tensor", "LX", x_loaded)

        def ln_inplace(wait_cv):
            P.wait("scalar", "CV", wait_cv)
            for tc in range(TC):
                xa = x_res[:, tc * D:(tc + 1) * D]
                rb = [bx(x_res, 0, 128, tc * D, (tc + 1) * D)]
                P.op("scalar", lambda e, xa=xa, tc=tc: e.activation(
                    jnk[:, :], xa, AF.Identity, accum_out=st_sx[:, tc:tc + 1]),
                    reads=rb, writes=[bx(jnk, 0, 128, 0, D), bx(st_sx, 0, 128, tc, tc + 1)])
                P.op("scalar", lambda e, xa=xa, tc=tc: e.activation(
                    jnk[:, :], xa, AF.Square, accum_out=st_sq[:, tc:tc + 1]),
                    reads=rb, writes=[bx(jnk, 0, 128, 0, D), bx(st_sq, 0, 128, tc, tc + 1)])
            self_d = P.op("scalar", lambda e: e.activation(jnk[:, 0:1], jnk[:, 0:1], AF.Identity),
                          reads=[bx(jnk, 0, 128, 0, 1)], writes=[bx(jnk, 0, 128, 0, 1)],
                          inc=("CS", 1))
            P.wait("vector", "CS", self_d)
            P.op("vector", lambda e: e.tensor_scalar(st_m[:, :], st_sx[:, :], 1.0 / D, None, ALU.mult),
                 reads=[bx(st_sx, 0, 128, 0, TC)], writes=[bx(st_m, 0, 128, 0, TC)])
            P.op("vector", lambda e: e.tensor_tensor(st_w[:, :], st_m[:, :], st_m[:, :], ALU.mult),
                 reads=[bx(st_m, 0, 128, 0, TC)], writes=[bx(st_w, 0, 128, 0, TC)])
            cv_var = P.op("vector", lambda e: e.scalar_tensor_tensor(
                st_w[:, :], st_sq[:, :], 1.0 / D, st_w[:, :], ALU.mult, ALU.subtract),
                reads=[bx(st_sq, 0, 128, 0, TC), bx(st_w, 0, 128, 0, TC)],
                writes=[bx(st_w, 0, 128, 0, TC)], inc=("CV", 1))
            P.wait("scalar", "CV", cv_var)
            P.op("scalar", lambda e: e.activation(st_w[:, :], st_w[:, :], AF.Ln,
                                                  bias=cb_eps[:, 0:1], scale=1.0),
                 reads=[bx(st_w, 0, 128, 0, TC)], writes=[bx(st_w, 0, 128, 0, TC)])
            cs_w = P.op("scalar", lambda e: e.activation(st_w[:, :], st_w[:, :], AF.Exp,
                                                         bias=0.0, scale=-0.5),
                        reads=[bx(st_w, 0, 128, 0, TC)], writes=[bx(st_w, 0, 128, 0, TC)],
                        inc=("CS", 1))
            P.wait("vector", "CS", cs_w)
            done = None
            for tc in range(TC):
                b = bx(x_res, 0, 128, tc * D, (tc + 1) * D)
                done = P.op("vector", lambda e, tc=tc: e.tensor_scalar(
                    x_res[:, tc * D:(tc + 1) * D], x_res[:, tc * D:(tc + 1) * D],
                    st_m[:, tc:tc + 1], st_w[:, tc:tc + 1], ALU.subtract, ALU.mult),
                    reads=[b, bx(st_m, 0, 128, tc, tc + 1), bx(st_w, 0, 128, tc, tc + 1)],
                    writes=[b], inc=("CV", 1))
            return done

        def build_xT(xT_hi, xT_lo, wait_cv, alt=True):
            par_cv = {0: wait_cv, 1: wait_cv}
            last = None
            for dc in range(DC):
                par = dc % 2 if alt else 0
                ps = psB if par == 0 else psA
                P.wait("tensor", "CV", par_cv[par])
                tl = [(ps[:, tc * 128:(tc + 1) * 128],
                       x_res[:, tc * D + dc * 128: tc * D + (dc + 1) * 128],
                       idf[:, :]) for tc in range(TC)]
                ct = P.trans(tl)
                P.wait("vector", "CT", ct)
                hi_b = bx(xT_hi, 0, 128, dc * T, (dc + 1) * T)
                lo_b = bx(xT_lo, 0, 128, dc * T, (dc + 1) * T)
                P.op("vector", lambda e, dc=dc, ps=ps: e.tensor_copy(
                    xT_hi[:, dc * T:(dc + 1) * T], ps[:, :]), writes=[hi_b])
                last = P.op("vector", lambda e, dc=dc, ps=ps: e.tensor_tensor(
                    xT_lo[:, dc * T:(dc + 1) * T], ps[:, :],
                    xT_hi[:, dc * T:(dc + 1) * T], ALU.subtract),
                    reads=[hi_b], writes=[lo_b], inc=("CV", 1))
                par_cv[par] = last
            return last

        def build_xT_f32(xTf, wait_cv):
            par_cv = {0: wait_cv, 1: wait_cv}
            last = None
            for dc in range(DC):
                par = dc % 2
                ps = psB if par == 0 else psA
                P.wait("tensor", "CV", par_cv[par])
                tl = [(ps[:, tc * 128:(tc + 1) * 128],
                       x_res[:, tc * D + dc * 128: tc * D + (dc + 1) * 128],
                       idf[:, :]) for tc in range(TC)]
                ct = P.trans(tl)
                P.wait("vector", "CT", ct)
                last = P.op("vector", lambda e, dc=dc, ps=ps: e.tensor_copy(
                    xTf[:, dc * T:(dc + 1) * T], ps[:, :]),
                    writes=[bx(xTf, 0, 128, dc * T, (dc + 1) * T)], inc=("CV", 1))
                par_cv[par] = last
            return last

        def attn_layer(layer, xTf, wait_cv):
            causal = (layer == 0)
            L = str(layer)
            with contextlib.ExitStack() as S:
                wp = [sb("a_wp0" + L, [128, 1024], F32, S),
                      sb("a_wp1" + L, [128, 1024], F32, S)]
                wohf = [sb("a_woh0" + L, [64, 1024], F32, S),
                        sb("a_woh1" + L, [64, 1024], F32, S)]
                kf = sb("a_kf" + L, [128, T], F32, S)
                qf = sb("a_qf" + L, [128, T], F32, S)
                Vst = [sb("a_Vst0" + L, [128, TC * 65], F32, S),
                       sb("a_Vst1" + L, [128, TC * 65], F32, S)]
                Ef = sb("a_Ef" + L, [128, 1024], F32, S)
                Efh = [sb("a_Efh0" + L, [128, 1024], F32, S),
                       sb("a_Efh1" + L, [128, 1024], F32, S)]
                cxf = [sb("a_cx0" + L, [64, T], F32, S),
                       sb("a_cx1" + L, [64, T], F32, S)]
                bqt = sb("a_bqt" + L, [128, 8], F32, S)
                bkt = sb("a_bkt" + L, [128, 8], F32, S)
                bvt = sb("a_bvt" + L, [128, 8], F32, S)

                P.op("vector", lambda e: e.memset(smalls[64:65, 0:64], 1.0),
                     writes=[bx(smalls, 64, 65, 0, 64)])

                cv_entry = P.cnt["CV"]
                cs_entry = P.cnt["CS"]
                P.wait("sync", "CV", P.cnt["CV"])
                P.wait("sync", "LW", P.cnt["LW"])
                P.wait("sync", "ST", P.cnt["ST"])
                P.wait("sync", "TS", P.cnt["TS"])
                for hp in range(8):
                    if layer == 0:
                        P.dma(bqt[:, hp:hp + 1], bass.AP(bq0, 2 * hp * DK, [[DK, 2], [1, DK]]), slow=True)
                        P.dma(bkt[:, hp:hp + 1], bass.AP(bk0, 2 * hp * DK, [[DK, 2], [1, DK]]), slow=True)
                    else:
                        P.dma(bqt[:, hp:hp + 1], bass.AP(bqkv1, 2 * hp * 192, [[192, 2], [1, DK]]), slow=True)
                        P.dma(bkt[:, hp:hp + 1], bass.AP(bqkv1, 2 * hp * 192 + DK, [[192, 2], [1, DK]]), slow=True)
                bsrc = bo0 if layer == 0 else bout1
                P.wait("sync", "CS", P.cnt["CS"])
                bias_ld = P.dma(btile[:, :], bass.AP(bsrc, 0, [[0, 128], [1, D]]), slow=True)
                P.wait("scalar", "LW", bias_ld)
                P.wait("vector", "LW", bias_ld)
                bt_done = None
                for tc in range(TC):
                    b = bx(x_res, 0, 128, tc * D, (tc + 1) * D)
                    bt_done = P.op("vector", lambda e, tc=tc: e.tensor_tensor(
                        x_res[:, tc * D:(tc + 1) * D], x_res[:, tc * D:(tc + 1) * D],
                        btile[:, :], ALU.add),
                        reads=[b, bx(btile, 0, 128, 0, D)], writes=[b], inc=("CV", 1))
                P.wait("tensor", "CV", bt_done)

                cs_exp_h = {0: 0, 1: 0}
                ct_pv_h = {0: 0, 1: 0}
                pair_free = {0: 0, 1: 0}
                add_done = {0: bt_done, 1: bt_done}
                ct_outproj_prev = 0
                cv_ctx_prev = cv_entry
                cs_psA_prev = cs_entry
                cv_msk = 0
                ct_pv = 0

                for hp in range(8):
                    def load_w(ti, t, par, hp=hp):
                        if pair_free[par] == 0:
                            P.wait("sync", "LW", P.cnt["LW"])
                        P.wait("sync", "CT", pair_free[par])
                        lw = None
                        for h2 in range(2):
                            if layer == 0:
                                srcw = {"q": wq0, "k": wk0, "v": wv0}[t]
                                in_ap = bass.AP(srcw, (2 * hp + h2) * D * DK,
                                                [[DK, 128], [128 * DK, DC], [1, DK]])
                            else:
                                in_ap = bass.AP(wqkv1, (2 * hp + h2) * 192 + ti * DK,
                                                [[3 * D, 128], [128 * 3 * D, DC], [1, DK]])
                            wst_ap = bass.AP(wp[par], h2 * 64, [[1024, 128], [128, DC], [1, 64]])
                            lw = P.dma(wst_ap, in_ap, slow=True)
                        return lw

                    def proj_fm(ps, par, lw, gate_cv, gate_cs):
                        P.wait("tensor", "CS", gate_cs)
                        P.wait("tensor", "CV", gate_cv)
                        P.wait("tensor", "LW", lw)
                        mml = []
                        for dc in range(DC):
                            for blk in range(4):
                                c0 = blk * 512
                                mml.append((ps[:, c0:c0 + 512],
                                            wp[par][:, dc * 128:(dc + 1) * 128],
                                            xTf[:, dc * T + c0: dc * T + c0 + 512],
                                            dc == 0, dc == DC - 1))
                        return P.mms(mml)

                    load_w(0, "q", 0)
                    load_w(1, "k", 1)
                    ct_q = proj_fm(psA, 0, P.cnt["LW"], cv_ctx_prev, cs_psA_prev)
                    pair_free[0] = ct_q
                    P.wait("scalar", "CT", ct_q)
                    cs_qb = P.op("scalar", lambda e, hp=hp: e.activation(
                        qf[:, :], psA[:, :], AF.Identity, bias=bqt[:, hp:hp + 1], scale=1.0),
                        writes=[bx(qf, 0, 128, 0, T)], inc=("CS", 1))
                    load_w(2, "v", 0)
                    ct_k = proj_fm(psB, 1, P.cnt["LW"], max(add_done[0], add_done[1]),
                                   cs_psA_prev)
                    pair_free[1] = ct_k
                    P.wait("scalar", "CT", ct_k)
                    cs_k = P.op("scalar", lambda e, hp=hp: e.activation(
                        kf[:, :], psB[:, :], AF.Identity, bias=bkt[:, hp:hp + 1], scale=1.0),
                        writes=[bx(kf, 0, 128, 0, T)], inc=("CS", 1))

                    wsrc = wo0 if layer == 0 else wout1
                    P.wait("sync", "LW", P.cnt["LW"])
                    P.wait("sync", "CT", ct_outproj_prev)
                    lw_wo = None
                    for h2 in range(2):
                        lw_wo = P.dma(bass.AP(wohf[h2], 0, [[1024, 64], [1, D]]),
                                      bass.AP(wsrc, (hp * 128 + h2 * 64) * D, [[D, 64], [1, D]]))

                    if layer == 0:
                        bv_ld = P.dma(bvt[:, hp:hp + 1], bass.AP(bv0, 2 * hp * DK, [[DK, 2], [1, DK]]),
                                      sem="LX", wait=("CV", P.cnt["CV"]), slow=True)
                    else:
                        bv_ld = P.dma(bvt[:, hp:hp + 1], bass.AP(bqkv1, 2 * hp * 192 + 2 * DK,
                                                                 [[192, 2], [1, DK]]),
                                      sem="LX", wait=("CV", P.cnt["CV"]), slow=True)

                    ct_v = proj_fm(psA, 0, P.cnt["LW"], cv_ctx_prev, cs_qb)
                    pair_free[0] = ct_v

                    P.wait("scalar", "CT", ct_v)
                    P.wait("scalar", "LX", bv_ld)
                    P.wait("scalar", "CV", cv_ctx_prev)
                    cs_v1 = P.op("scalar", lambda e, hp=hp: e.activation(
                        Ef[:, :], psA[:, 0:1024], AF.Identity, bias=bvt[:, hp:hp + 1], scale=1.0),
                        writes=[bx(Ef, 0, 128, 0, 1024)], inc=("CS", 1))
                    P.wait("tensor", "CS", cs_v1)
                    tl = [(psB[:, i * 128:(i + 1) * 128],
                           Ef[:, i * 128:(i + 1) * 128], idf[:, :]) for i in range(8)]
                    ct_t1 = P.trans(tl)
                    P.wait("scalar", "CT", ct_t1)
                    cs_v2 = P.op("scalar", lambda e, hp=hp: e.activation(
                        Ef[:, :], psA[:, 1024:2048], AF.Identity, bias=bvt[:, hp:hp + 1], scale=1.0),
                        writes=[bx(Ef, 0, 128, 0, 1024)], inc=("CS", 1))
                    P.wait("tensor", "CS", cs_v2)
                    tl = [(psB[:, 1024 + i * 128:1024 + (i + 1) * 128],
                           Ef[:, i * 128:(i + 1) * 128], idf[:, :]) for i in range(8)]
                    ct_t2 = P.trans(tl)

                    P.wait("vector", "CT", ct_t2)
                    cv_vst = None
                    for h2 in range(2):
                        r0 = h2 * 64
                        P.op("vector", lambda e, r0=r0, h2=h2: e.tensor_copy(
                            bass.AP(Vst[h2], 0, [[TC * 65, 128], [65, TC], [1, 64]]),
                            bass.AP(psB, r0, [[2048, 128], [128, TC], [1, 64]])),
                            writes=[bx(Vst[h2], 0, 128, 0, TC * 65)])
                        cv_vst = P.op("vector", lambda e, h2=h2: e.memset(
                            bass.AP(Vst[h2], 64, [[TC * 65, 128], [65, TC], [1, 1]]), 1.0),
                            writes=[bx(Vst[h2], 0, 128, 0, TC * 65)], inc=("CV", 1))

                    P.wait("tensor", "CV", cv_vst)
                    for h2 in range(2):
                        r0, r1 = h2 * 64, (h2 + 1) * 64
                        for j in range(TC):
                            qb0 = 512 * (j // 4) if causal else 0
                            live0 = j * 128 if causal else 0
                            halves = [hh for hh in range(2)
                                      if max(qb0, hh * 1024) < (hh + 1) * 1024]
                            ct_sc_h = {}
                            for hh in halves:
                                lo_a = max(qb0, hh * 1024)
                                hi_a = (hh + 1) * 1024
                                P.wait("tensor", "CS", cs_exp_h[hh])
                                mml = [(psB[:, c0:c0 + 512],
                                        kf[r0:r1, j * 128:(j + 1) * 128],
                                        qf[r0:r1, c0:c0 + 512], True, True)
                                       for c0 in range(lo_a, hi_a, 512)]
                                ct_sc_h[hh] = P.mms(mml)
                            cv_mask = 0
                            if causal:
                                dh = (j * 128) // 1024
                                P.wait("vector", "CT", ct_sc_h[dh])
                                cv_mask = P.op("vector", lambda e, j=j: e.tensor_tensor(
                                    psB[:, j * 128:(j + 1) * 128],
                                    psB[:, j * 128:(j + 1) * 128],
                                    dmask[:, :], ALU.add),
                                    reads=[bx(dmask, 0, 128, 0, 128)],
                                    writes=[bx(psB, 0, 128, j * 128, (j + 1) * 128)],
                                    inc=("CV", 1))
                            for hh in halves:
                                lo_a = max(qb0, hh * 1024)
                                hi_a = (hh + 1) * 1024
                                lv_a = max(live0, lo_a)
                                lo_l, lv_l = lo_a - hh * 1024, lv_a - hh * 1024
                                Eh = Efh[hh]
                                if causal and (j * 128) // 1024 == hh:
                                    P.wait("scalar", "CV", cv_mask)
                                P.wait("scalar", "CT", ct_sc_h[hh])
                                P.wait("scalar", "CT", ct_pv_h[hh])
                                cs_exp_h[hh] = P.op("scalar", lambda e, lv_a=lv_a, hi_a=hi_a,
                                                    lv_l=lv_l, Eh=Eh: e.activation(
                                    Eh[:, lv_l:1024], psB[:, lv_a:hi_a], AF.Exp,
                                    bias=0.0, scale=0.125),
                                    writes=[bx(Eh, 0, 128, lv_l, 1024)], inc=("CS", 1))
                                if lv_l > lo_l:
                                    P.wait("vector", "CT", ct_pv_h[hh])
                                    cv_msk = P.op("vector", lambda e, lo_l=lo_l, lv_l=lv_l,
                                                  Eh=Eh: e.memset(Eh[:, lo_l:lv_l], 0.0),
                                                  writes=[bx(Eh, 0, 128, lo_l, lv_l)],
                                                  inc=("CV", 1))
                                    P.wait("tensor", "CV", cv_msk)
                                P.wait("tensor", "CS", cs_exp_h[hh])
                                va = Vst[h2][:, j * 65:(j + 1) * 65]
                                mml = []
                                for c0 in range(lo_a, hi_a, 512):
                                    qbi = c0 // 512
                                    lastj = min(4 * qbi + 3, TC - 1) if causal else TC - 1
                                    cl = c0 - hh * 1024
                                    mml.append((psA[0:65, c0:c0 + 512], va,
                                                Eh[:, cl:cl + 512],
                                                j == 0, j == lastj))
                                ct_pv_h[hh] = P.mms(mml)
                                ct_pv = ct_pv_h[hh]
                        P.wait("vector", "CT", ct_pv)
                        P.wait("vector", "CT", ct_outproj_prev)
                        cv_ctx = None
                        P.wait("scalar", "CT", ct_pv)
                        for hh in range(2):
                            a0 = hh * 1024
                            P.wait("scalar", "CT", P.cnt["CT"])
                            P.op("scalar", lambda e, a0=a0: e.activation(
                                Ef[64:65, 0:1024], psA[64:65, a0:a0 + 1024], AF.Ln),
                                writes=[bx(Ef, 64, 65, 0, 1024)])
                            cs_re = P.op("scalar", lambda e: e.activation(
                                Ef[64:65, 0:1024], Ef[64:65, 0:1024], AF.Exp,
                                bias=0.0, scale=-1.0),
                                reads=[bx(Ef, 64, 65, 0, 1024)],
                                writes=[bx(Ef, 64, 65, 0, 1024)], inc=("CS", 1))
                            cv_ecp = P.op("vector", lambda e, a0=a0: e.tensor_copy(
                                Ef[0:64, 0:1024], psA[0:64, a0:a0 + 1024]),
                                writes=[bx(Ef, 0, 64, 0, 1024)], inc=("CV", 1))
                            P.wait("tensor", "CS", cs_re)
                            P.wait("tensor", "CV", cv_ecp)
                            mml = [(psA[0:64, a0 + c0:a0 + c0 + 512], smalls[64:65, 0:64],
                                    Ef[64:65, c0:c0 + 512], True, True) for c0 in (0, 512)]
                            ct_bc = P.mms(mml)
                            P.wait("vector", "CT", ct_bc)
                            cv_ctx = P.op("vector", lambda e, h2=h2, a0=a0: e.tensor_tensor(
                                cxf[h2][0:64, a0:a0 + 1024], Ef[0:64, 0:1024],
                                psA[0:64, a0:a0 + 1024], ALU.mult),
                                reads=[bx(Ef, 0, 64, 0, 1024)],
                                writes=[bx(cxf[h2], 0, 64, a0, a0 + 1024)], inc=("CV", 1))
                        P.wait("tensor", "CV", cv_ctx)

                    P.wait("tensor", "CV", cv_ctx)
                    P.wait("tensor", "CS", P.cnt["CS"])
                    P.wait("tensor", "LW", P.cnt["LW"])
                    for tc in range(TC):
                        reg0 = (tc % 2) * 1024
                        P.wait("tensor", "CV", add_done[tc % 2])
                        mml = []
                        for h2 in range(2):
                            for c0 in range(0, 1024, 512):
                                mml.append((psB[:, reg0 + c0:reg0 + c0 + 512],
                                            cxf[h2][0:64, tc * 128:(tc + 1) * 128],
                                            wohf[h2][0:64, c0:c0 + 512],
                                            h2 == 0, h2 == 1))
                        ct_op = P.mms(mml)
                        P.wait("vector", "CT", ct_op)
                        b = bx(x_res, 0, 128, tc * D, (tc + 1) * D)
                        add_done[tc % 2] = P.op("vector", lambda e, tc=tc, reg0=reg0: e.tensor_tensor(
                            x_res[:, tc * D:(tc + 1) * D], x_res[:, tc * D:(tc + 1) * D],
                            psB[:, reg0:reg0 + 1024], ALU.add),
                            reads=[b], writes=[b], inc=("CV", 1))
                        ct_outproj_prev = ct_op
                    cv_ctx_prev = cv_ctx
                    cs_psA_prev = P.cnt["CS"]
                return ln_inplace(P.cnt["CV"])

        def ternarize_weights(S):
            if True:
                wstg = [sb("t_wstg%d" % i, [128, 1024], F32, S) for i in range(4)]
                bfb = [sb("t_bfb0", [128, 1024], BF16, S), sb("t_bfb1", [128, 1024], BF16, S)]
                geb = sb("t_geb", [128, 1024], F32, S)
                hbc = sb("t_hbc", [128, 1], F32, S)
                hbn = sb("t_hbn", [128, 1], F32, S)
                tbb = [sb("t_tb0", [128, 1024], BF16, S), sb("t_tb1", [128, 1024], BF16, S)]

                P.wait("sync", "LW", P.cnt["LW"])
                P.wait("sync", "CV", P.cnt["CV"])
                sts = []
                st_tb = {0: 0, 1: 0}

                def transpose_group(wi, k, eng, sls, sst, ssem):
                    gate = P.cnt[ssem]
                    for j in range(8):
                        b = j % 2
                        if wi == 0:
                            dc, r0 = j, k * 1024
                            in_ap = bass.AP(w1t_d, dc * 128 + r0 * D, [[D, 1024], [1, 128]])
                        else:
                            fc = k * 8 + j
                            in_ap = bass.AP(w2t_d, fc * 128, [[FF, D], [1, 128]])
                            out_ap = bass.AP(w2T_d, fc * 128 * D, [[D, 128], [1, D]])
                        P.wait(eng, ssem, gate)
                        P.wait(eng, sst, st_tb[b])
                        lw = P.dmat(tbb[b][:, :], in_ap, sem=sls, eng=eng)
                        P.wait(eng, sst, P.cnt[sst])
                        if wi == 0:
                            for s in range(4):
                                fcp = r0 // 256 + s
                                out_ap = bass.AP(w1T_d, fcp * 128 * 2048 + dc * 256,
                                                 [[2048, 128], [1, 256]])
                                st_tb[b] = P.dma(out_ap, tbb[b][:, s * 256:(s + 1) * 256],
                                                 sem=sst, eng=eng,
                                                 wait=(sls, lw) if s == 0 else None)
                        else:
                            st_tb[b] = P.dma(out_ap, tbb[b][:, :], sem=sst, eng=eng,
                                             wait=(sls, lw))

                cv_h_w2 = 0
                for wi in range(2):
                    ps = psA if wi == 0 else psB
                    if wi == 0:
                        src, dst, tot, rs = w1, w1t_d, FF * D, D
                        chunks = [ch * 128 * D for ch in range(32)]
                        eng, sls, sst, ssem = "sync", "LW", "TS", "ST"
                    else:
                        src, dst, tot, rs = w2, w2t_d, D * FF, FF
                        chunks = [r * 128 * FF + c * 1024
                                  for c in range(4) for r in range(8)]
                        eng, sls, sst, ssem = "gpsimd", "GX", "TS2", "LY"
                    P.wait("vector", "ST", P.cnt["ST"])
                    ct_mm = 0
                    gate0 = ("CS", P.cnt["CS"]) if wi == 0 else ("CV", P.cnt["CV"])
                    npair = len(chunks) // 2
                    cv_cp = {q: P.cnt["CV"] for q in range(4)}
                    def emit_p1_dmas(k):
                        lw = None
                        for b in range(2):
                            i = 2 * k + b
                            q = i % 4
                            g = gate0 if k == 0 else (
                                None if b == 1 else
                                ("CV", max(cv_cp[i % 4], cv_cp[(i + 1) % 4])))
                            lw = P.dma(wstg[q][:, :],
                                       bass.AP(src, chunks[i], [[rs, 128], [1, 1024]]),
                                       eng=eng, sem=sls, wait=g)
                        return lw
                    emit_p1_dmas(0)
                    for k in range(npair):
                        if k + 1 < npair:
                            emit_p1_dmas(k + 1)
                        P.wait("scalar", sls, P.cnt[sls])
                        cs_abs = None
                        for b in range(2):
                            q = (2 * k + b) % 4
                            cs_abs = P.op("scalar", lambda e, q=q: e.activation(
                                wstg[q][:, :], wstg[q][:, :], AF.Abs),
                                reads=[bx(wstg[q], 0, 128, 0, 1024)],
                                writes=[bx(wstg[q], 0, 128, 0, 1024)], inc=("CS", 1))
                        P.wait("vector", "CS", cs_abs)
                        P.wait("vector", "CT", ct_mm)
                        for b in range(2):
                            q = (2 * k + b) % 4
                            cv_cp[q] = P.op("vector", lambda e, q=q, b=b: e.tensor_copy(
                                bfb[b][:, :], wstg[q][:, :]),
                                reads=[bx(wstg[q], 0, 128, 0, 1024)],
                                writes=[bx(bfb[b], 0, 128, 0, 1024)], inc=("CV", 1))
                        P.wait("tensor", "CV", cv_cp[(2 * k + 1) % 4])
                        mml = []
                        for b in range(2):
                            for c0 in (0, 512):
                                mml.append((ps[0:1, 1536:2048], ones128[:, 0:1],
                                            bfb[b][:, c0:c0 + 512],
                                            k == 0 and b == 0 and c0 == 0,
                                            k == npair - 1 and b == 1 and c0 == 512))
                        ct_mm = P.mms(mml)
                    ct_sum = P.cnt["CT"]
                    P.wait("vector", "CT", ct_sum)
                    sc = 8 + wi
                    P.op("vector", lambda e, sc=sc, ps=ps: e.reduce_sum(
                        smalls[0:1, sc:sc + 1], ps[0:1, 1536:2048], AX.X),
                        writes=[bx(smalls, 0, 1, sc, sc + 1)])
                    cv_s = P.op("vector", lambda e, sc=sc, tot=tot: e.tensor_scalar(
                        smalls[0:1, sc:sc + 1], smalls[0:1, sc:sc + 1],
                        1.0 / tot, 1e-8, ALU.mult, ALU.max),
                        reads=[bx(smalls, 0, 1, sc, sc + 1)],
                        writes=[bx(smalls, 0, 1, sc, sc + 1)], inc=("CV", 1))
                    P.wait("tensor", "CV", cv_s)
                    ct_b = P.mms([(ps[:, 0:1], onescol[0:1, :], smalls[0:1, sc:sc + 1],
                                   True, True)])
                    P.wait("vector", "CT", ct_b)
                    P.op("vector", lambda e, ps=ps: e.tensor_scalar(
                        hbc[:, :], ps[:, 0:1], 0.5, None, ALU.mult),
                        writes=[bx(hbc, 0, 128, 0, 1)])
                    cv_h = P.op("vector", lambda e, ps=ps: e.tensor_scalar(
                        hbn[:, :], ps[:, 0:1], -0.5, None, ALU.mult),
                        writes=[bx(hbn, 0, 128, 0, 1)], inc=("CV", 1))
                    if wi == 1:
                        cv_h_w2 = cv_h
                    cv_tern = {q: cv_h for q in range(4)}
                    st_store = 0
                    def emit_p2_dmas(k):
                        for b in range(2):
                            i = 2 * k + b
                            q = i % 4
                            g = ("CV", max(cv_tern[i % 4], cv_tern[(i + 1) % 4]))
                            if b == 1 and wi == 0:
                                P.dma(wstg[q][:, :],
                                      bass.AP(src, chunks[i], [[rs, 128], [1, 1024]]),
                                      eng="scalar", sem="LS", wait=g)
                            else:
                                P.dma(wstg[q][:, :],
                                      bass.AP(src, chunks[i], [[rs, 128], [1, 1024]]),
                                      eng=eng, sem=sls,
                                      wait=g if b == 0 else None)
                    emit_p2_dmas(0)
                    for k in range(npair):
                        if k + 1 < npair:
                            emit_p2_dmas(k + 1)
                        P.wait("vector", sls, P.cnt[sls])
                        if wi == 0:
                            P.wait("vector", "LS", P.cnt["LS"])
                        P.wait("vector", "CV", cv_h)
                        P.wait("vector", ssem, st_store)
                        for b in range(2):
                            q = (2 * k + b) % 4
                            P.op("vector", lambda e, q=q: e.tensor_scalar(
                                geb[:, :], wstg[q][:, :], hbc[:, 0:1], None, ALU.is_ge),
                                reads=[bx(wstg[q], 0, 128, 0, 1024), bx(hbc, 0, 128, 0, 1)],
                                writes=[bx(geb, 0, 128, 0, 1024)])
                            P.op("vector", lambda e, q=q, b=b: e.tensor_scalar(
                                bfb[b][:, :], wstg[q][:, :], hbn[:, 0:1], None, ALU.is_le),
                                reads=[bx(wstg[q], 0, 128, 0, 1024), bx(hbn, 0, 128, 0, 1)],
                                writes=[bx(bfb[b], 0, 128, 0, 1024)])
                            cv_tern[q] = P.op("vector", lambda e, b=b: e.tensor_tensor(
                                bfb[b][:, :], geb[:, :], bfb[b][:, :], ALU.subtract),
                                reads=[bx(geb, 0, 128, 0, 1024), bx(bfb[b], 0, 128, 0, 1024)],
                                writes=[bx(bfb[b], 0, 128, 0, 1024)], inc=("CV", 1))
                        for b in range(2):
                            q = (2 * k + b) % 4
                            st_store = P.dma(bass.AP(dst, chunks[2 * k + b],
                                                     [[rs, 128], [1, 1024]]),
                                             bfb[b][:, :], sem=ssem, eng=eng,
                                             wait=("CV", cv_tern[q]))
                        if wi == 0 and k % 4 == 3:
                            transpose_group(wi, k // 4, eng, sls, sst, ssem)
                    if wi == 0:
                        sts.append(max(st_tb[0], st_tb[1]))

                def w2rt(k):
                    transpose_group(1, k, "sync", "LZ", "TS", "LY")
            return {"w1_ts": sts[0], "cv_h_w2": cv_h_w2, "w2rt": w2rt}

        def ffn0_layer(xT_hi, xT_lo, tern_st, gates):
            TB = 256
            with contextlib.ExitStack() as S:
                w1tp = [sb("f_w1a", [128, 2048], BF16, S), sb("f_w1b", [128, 2048], BF16, S)]
                w2tp = [sb("f_w2a", [128, 1024], BF16, S), sb("f_w2b", [128, 1024], BF16, S)]
                hT_hi = sb("f_hthi", [128, 32 * TB], BF16, S)
                hT_lo = sb("f_htlo", [128, 32 * TB], BF16, S)
                hg = sb("f_hg", [128, TB], F32, S)
                b1t = sb("f_b1t", [128, 32], F32, S)

                P.wait("sync", "CV", gates["cv"])
                P.wait("sync", "LW", gates["lw"])
                P.dma(b1t[:, :], bass.AP(b1, 0, [[1, 128], [128, 32]]), slow=True)
                P.wait("sync", "CS", gates["cs"])
                bld = P.dma(btile[:, :], bass.AP(b2, 0, [[0, 128], [1, D]]), slow=True)
                P.wait("scalar", "LW", bld)
                P.wait("vector", "LW", bld)
                bt_done = None
                for tc in range(TC):
                    b = bx(x_res, 0, 128, tc * D, (tc + 1) * D)
                    bt_done = P.op("vector", lambda e, tc=tc: e.tensor_tensor(
                        x_res[:, tc * D:(tc + 1) * D], x_res[:, tc * D:(tc + 1) * D],
                        btile[:, :], ALU.add),
                        reads=[b, bx(btile, 0, 128, 0, D)], writes=[b], inc=("CV", 1))

                lw1 = {0: 0, 1: 0}
                use1 = {0: 0, 1: 0}
                lw2 = {0: 0, 1: 0}
                use2 = {0: 0, 1: 0}
                cs_g_reg = [0] * 4
                cv_h_prev = 0
                w2_ts_ready = 0
                for tb in range(8):
                    for fcp in range(16):
                        if tb == 0 and fcp % 4 == 3:
                            tern_st["w2rt"](fcp // 4)
                            w2_ts_ready = P.cnt["TS"]
                        b = fcp % 2
                        in_ap = bass.AP(w1T_d, fcp * 128 * 2048, [[2048, 128], [1, 2048]])
                        out_ap = bass.AP(w1tp[b], 0, [[2048, 128], [1, 2048]])
                        if tb == 0 and fcp < 2:
                            P.wait("sync", "TS", tern_st["w1_ts"])
                        lw1[b] = P.dma(out_ap, in_ap, sem="LA%d" % b,
                                       wait=("CT", use1[b]))
                        P.wait("tensor", "LA%d" % b, lw1[b])
                        for sub in range(2):
                            fc = fcp * 2 + sub
                            reg0 = (fc % 4) * 512
                            P.wait("tensor", "CS", cs_g_reg[fc % 4])
                            mml = []
                            for dc in range(DC):
                                for pi, part in enumerate((xT_hi, xT_lo)):
                                    mml.append((psA[:, reg0:reg0 + TB],
                                                w1tp[b][:, dc * 256 + sub * 128:
                                                        dc * 256 + sub * 128 + 128],
                                                part[:, dc * T + tb * TB: dc * T + tb * TB + TB],
                                                dc == 0 and pi == 0,
                                                dc == DC - 1 and pi == 1))
                            ct_h = P.mms(mml)
                            use1[b] = ct_h
                            P.wait("scalar", "CT", ct_h)
                            P.wait("scalar", "CV", cv_h_prev)
                            cs_g = P.op("scalar", lambda e, fc=fc, reg0=reg0: e.activation(
                                hg[:, :], psA[:, reg0:reg0 + TB], AF.Gelu,
                                bias=b1t[:, fc:fc + 1], scale=1.0),
                                writes=[bx(hg, 0, 128, 0, TB)], inc=("CS", 1))
                            cs_g_reg[fc % 4] = cs_g
                            P.wait("vector", "CS", cs_g)
                            P.op("vector", lambda e, fc=fc: e.tensor_copy(
                                hT_hi[:, fc * TB:(fc + 1) * TB], hg[:, :]),
                                reads=[bx(hg, 0, 128, 0, TB)],
                                writes=[bx(hT_hi, 0, 128, fc * TB, (fc + 1) * TB)])
                            cv_h_prev = P.op("vector", lambda e, fc=fc: e.tensor_tensor(
                                hT_lo[:, fc * TB:(fc + 1) * TB], hg[:, :],
                                hT_hi[:, fc * TB:(fc + 1) * TB], ALU.subtract),
                                reads=[bx(hg, 0, 128, 0, TB),
                                       bx(hT_hi, 0, 128, fc * TB, (fc + 1) * TB)],
                                writes=[bx(hT_lo, 0, 128, fc * TB, (fc + 1) * TB)],
                                inc=("CV", 1))
                    cv_h_done = cv_h_prev
                    P.wait("tensor", "CV", cv_h_done)
                    ct_f2 = None
                    for fc in range(32):
                        b = fc % 2
                        if tb == 0 and fc < 2:
                            P.wait("sync", "TS", w2_ts_ready)
                        if tb == 0 and fc == 0:
                            P.wait("tensor", "CV", tern_st["cv_h_w2"])
                        lw2[b] = P.dma(w2tp[b][:, :],
                                       bass.AP(w2T_d, fc * 128 * D, [[D, 128], [1, D]]),
                                       sem="LB%d" % b, wait=("CT", use2[b]))
                        P.wait("tensor", "LB%d" % b, lw2[b])
                        mml = []
                        for tcl in range(2):
                            reg0 = tcl * 1024
                            for c0 in range(0, 1024, 512):
                                for pi, hh in enumerate((hT_hi, hT_lo)):
                                    mml.append((psB[:, reg0 + c0:reg0 + c0 + 512],
                                                hh[:, fc * TB + tcl * 128: fc * TB + tcl * 128 + 128],
                                                w2tp[b][:, c0:c0 + 512],
                                                fc == 0 and pi == 0,
                                                fc == 31 and pi == 1))
                        ct_f2 = P.mms(mml)
                        use2[b] = ct_f2
                    P.wait("vector", "CT", ct_f2)
                    for tcl in range(2):
                        tc = tb * 2 + tcl
                        bxx = bx(x_res, 0, 128, tc * D, (tc + 1) * D)
                        P.op("vector", lambda e, tc=tc, tcl=tcl: e.tensor_tensor(
                            x_res[:, tc * D:(tc + 1) * D], x_res[:, tc * D:(tc + 1) * D],
                            psB[:, tcl * 1024:tcl * 1024 + 1024], ALU.add),
                            reads=[bxx], writes=[bxx], inc=("CV", 1))
                return ln_inplace(P.cnt["CV"])

        def ffn1_layer():
            with contextlib.ExitStack() as S:
                W = 2048
                kq = sb("q_kq", [128, DC * 1024], BF16, S)
                kstg = sb("q_kstg", [128, 1024], BF16, S)
                k8 = sb("q_k8", [128, 1024], FP8, S)
                kint = sb("q_kint", [128, 1024], I8, S)
                g0 = sb("q_g0", [128, W], F32, S)
                g1 = sb("q_g1", [128, W], F32, S)
                g2 = sb("q_g2", [128, W], F32, S)
                g3 = sb("q_g3", [128, W], F32, S)
                g4 = sb("q_g4", [128, W], F32, S)
                g5 = sb("q_g5", [128, W], F32, S)
                gax = sb("q_gax", [128, W], F32, S)
                gsg = sb("q_gsg", [128, W], F32, S)
                hwb = sb("q_hwb", [128, W], BF16, S)
                hw8 = sb("q_hw8", [128, W], FP8, S)
                hww = sb("q_hww", [128, W], F32, S)
                xa = sb("q_xa", [128, W], F32, S)
                xqT = sb("q_xqT", [128, W], BF16, S)
                yq = sb("q_yq", [128, W], F32, S)
                w01 = sb("q_w01", [128, 1], F32, S)
                ssbc = sb("q_ssbc", [128, 1], F32, S)
                rsibc = sb("q_rsibc", [128, 1], F32, S)
                bl2 = sb("q_bl2", [128, W], BF16, S)

                def vop(fn, reads, writes, inc=None):
                    return P.op("vector", fn, reads=reads, writes=writes, inc=inc)

                def B(t, f0=0, f1=None):
                    return bx(t, 0, 128, f0, W if f1 is None else f1)

                P.wait("sync", "CV", P.cnt["CV"])
                P.wait("sync", "LW", P.cnt["LW"])
                P.dma(smalls[0:1, 40:41], bass.AP(sin1, 0, [[1, 1], [1, 1]]), slow=True)
                P.dma(smalls[0:1, 41:42], bass.AP(sk1, 0, [[1, 1], [1, 1]]), slow=True)
                P.dma(bl2[:, 0:1024], bass.AP(bias1, 0, [[0, 128], [1, D]]), slow=True)
                bld = P.dma(bl2[:, 1024:2048], bass.AP(bias1, 0, [[0, 128], [1, D]]), slow=True)
                P.wait("vector", "LW", bld)
                P.op("vector", lambda e: e.tensor_tensor(
                    smalls[0:1, 42:43], smalls[0:1, 40:41], smalls[0:1, 41:42], ALU.mult),
                    reads=[bx(smalls, 0, 1, 40, 42)], writes=[bx(smalls, 0, 1, 42, 43)])
                P.op("vector", lambda e: e.reciprocal(smalls[0:1, 43:44], smalls[0:1, 41:42]),
                     reads=[bx(smalls, 0, 1, 41, 42)], writes=[bx(smalls, 0, 1, 43, 44)])
                cv_sc = P.op("vector", lambda e: e.reciprocal(smalls[0:1, 44:45], smalls[0:1, 40:41]),
                             reads=[bx(smalls, 0, 1, 40, 41)], writes=[bx(smalls, 0, 1, 44, 45)],
                             inc=("CV", 1))
                P.wait("tensor", "CV", cv_sc)
                ct_bc = P.mms([(psA[:, 0:1], onescol[0:1, :], smalls[0:1, 42:43], True, True),
                               (psA[:, 1:2], onescol[0:1, :], smalls[0:1, 44:45], True, True),
                               (psA[:, 2:3], onescol[0:1, :], smalls[0:1, 43:44], True, True)])
                P.wait("vector", "CT", ct_bc)
                P.op("vector", lambda e: e.tensor_copy(ssbc[:, :], psA[:, 0:1]),
                     writes=[bx(ssbc, 0, 128, 0, 1)])
                P.op("vector", lambda e: e.tensor_copy(rsibc[:, :], psA[:, 1:2]),
                     writes=[bx(rsibc, 0, 128, 0, 1)])
                cv_bc2 = P.op("vector", lambda e: e.tensor_copy(smalls[:, 45:46], psA[:, 2:3]),
                              writes=[bx(smalls, 0, 128, 45, 46)], inc=("CV", 1))

                P.op("vector", lambda e: e.memset(w01[:, :], 2.0 ** -30),
                     writes=[bx(w01, 0, 128, 0, 1)])
                P.op("vector", lambda e: e.tensor_scalar(
                    w01[:, :], w01[:, :], 1.0, 1.0, ALU.add, ALU.subtract),
                    reads=[bx(w01, 0, 128, 0, 1)], writes=[bx(w01, 0, 128, 0, 1)])
                P.op("vector", lambda e: e.tensor_scalar(
                    w01[:, :], w01[:, :], 2.0 ** -31, None, ALU.is_ge),
                    reads=[bx(w01, 0, 128, 0, 1)], writes=[bx(w01, 0, 128, 0, 1)])

                kq_done = None
                for dc in range(DC):
                    lw = P.dma(kstg[:, :], bass.AP(kern1, dc * 128 * D, [[D, 128], [1, D]]),
                               wait=("CV", P.cnt["CV"]))
                    P.wait("vector", "LW", lw)
                    P.op("vector", lambda e: e.tensor_scalar(
                        g0[:, 0:1024], kstg[:, :], smalls[:, 45:46], None, ALU.mult),
                        reads=[bx(kstg, 0, 128, 0, 1024), bx(smalls, 0, 128, 45, 46)],
                        writes=[bx(g0, 0, 128, 0, 1024)])
                    P.op("vector", lambda e: e.tensor_copy(k8[:, :], g0[:, 0:1024]),
                         reads=[bx(g0, 0, 128, 0, 1024)], writes=[bx(k8, 0, 128, 0, 1024)])
                    P.op("vector", lambda e: e.tensor_copy(kint[:, :], k8[:, :].bitcast(I8)),
                         reads=[bx(k8, 0, 128, 0, 1024)], writes=[bx(kint, 0, 128, 0, 1024)])
                    kq_done = P.op("vector", lambda e, dc=dc: e.tensor_copy(
                        kq[:, dc * 1024:(dc + 1) * 1024], kint[:, :].bitcast(FP8)),
                        reads=[bx(kint, 0, 128, 0, 1024)],
                        writes=[bx(kq, 0, 128, dc * 1024, (dc + 1) * 1024)], inc=("CV", 1))

                def exp_chain(clamp_lo, bias_exp):
                    vop(lambda e: e.tensor_scalar(
                        g1[:, :], g0[:, :], INVLN2, 511.5, ALU.mult, ALU.add),
                        [B(g0)], [B(g1)])
                    vop(lambda e: e.tensor_scalar(
                        g1[:, :], g1[:, :], B3, B3, ALU.add, ALU.subtract),
                        [B(g1)], [B(g1)])
                    vop(lambda e, c=float(clamp_lo): e.tensor_scalar(
                        g1[:, :], g1[:, :], c, 560.0, ALU.max, ALU.min),
                        [B(g1)], [B(g1)])
                    vop(lambda e, b=float(bias_exp): e.tensor_scalar(
                        g1[:, :], g1[:, :], -1.0, b, ALU.mult, ALU.add),
                        [B(g1)], [B(g1)])
                    vop(lambda e: e.tensor_scalar(
                        g2[:, :], g1[:, :], LH, None, ALU.mult),
                        [B(g1)], [B(g2)])
                    return vop(lambda e: e.scalar_tensor_tensor(
                        g1[:, :], g1[:, :], LL, g2[:, :], ALU.mult, ALU.add),
                        [B(g1), B(g2)], [B(g1)], inc=("CV", 1))

                def abs_sign_ln(src_ap, src_bx):
                    P.op("scalar", lambda e: e.activation(gax[:, :], src_ap, AF.Abs),
                         reads=[src_bx], writes=[B(gax)])
                    P.op("scalar", lambda e: e.activation(gsg[:, :], src_ap, AF.Sign),
                         reads=[src_bx], writes=[B(gsg)])
                    return P.op("scalar", lambda e: e.activation(
                        g0[:, :], gax[:, :], AF.Ln, bias=cb_tiny[:, 0:1], scale=1.0),
                        reads=[B(gax)], writes=[B(g0)], inc=("CS", 1))

                def exp_pair():
                    P.op("scalar", lambda e: e.activation(
                        g2[:, :], g1[:, :], AF.Exp, bias=0.0, scale=1.0),
                        reads=[B(g1)], writes=[B(g2)])
                    return P.op("scalar", lambda e: e.activation(
                        g3[:, :], g1[:, :], AF.Exp, bias=0.0, scale=-1.0),
                        reads=[B(g1)], writes=[B(g3)], inc=("CS", 1))

                P.wait("vector", "LW", bld)
                cv_xqT_prev = P.cnt["CV"]
                cs_y_prev = P.cnt["CS"]
                cv_res_prev = P.cnt["CV"]
                for p in range(8):
                    X = x_res[:, p * W:(p + 1) * W]
                    Xb = bx(x_res, 0, 128, p * W, (p + 1) * W)
                    P.wait("scalar", "CV", cv_res_prev)
                    cs1 = abs_sign_ln(X, Xb)
                    P.wait("vector", "CS", cs1)
                    cv1 = exp_chain(440, 535)
                    P.wait("scalar", "CV", cv1)
                    cs2 = exp_pair()
                    P.wait("vector", "CS", cs2)
                    vop(lambda e: e.tensor_tensor(g0[:, :], gax[:, :], g2[:, :], ALU.mult),
                        [B(gax), B(g2)], [B(g0)])
                    vop(lambda e: e.tensor_scalar(
                        g1[:, :], g0[:, :], B3, B3, ALU.add, ALU.subtract),
                        [B(g0)], [B(g1)])
                    vop(lambda e: e.tensor_scalar(
                        g4[:, :], g1[:, :], 2.0 ** -16, None, ALU.mult),
                        [B(g1)], [B(g4)])
                    vop(lambda e: e.tensor_scalar(
                        g4[:, :], g4[:, :], B3, B3, ALU.add, ALU.subtract),
                        [B(g4)], [B(g4)])
                    vop(lambda e: e.tensor_tensor(g4[:, :], g4[:, :], g3[:, :], ALU.mult),
                        [B(g4), B(g3)], [B(g4)])
                    vop(lambda e: e.tensor_tensor(g4[:, :], g4[:, :], gsg[:, :], ALU.mult),
                        [B(g4), B(gsg)], [B(g4)])
                    vop(lambda e: e.tensor_scalar(
                        g4[:, :], g4[:, :], 65536.0, None, ALU.mult),
                        [B(g4)], [B(g4)])
                    vop(lambda e, X=X: e.tensor_copy(hwb[:, :], X), [Xb], [B(hwb)])
                    vop(lambda e: e.tensor_copy(hww[:, :], hwb[:, :]), [B(hwb)], [B(hww)])
                    vop(lambda e: e.tensor_tensor(g5[:, :], g4[:, :], hww[:, :], ALU.subtract),
                        [B(g4), B(hww)], [B(g5)])
                    vop(lambda e: e.tensor_scalar(
                        g5[:, :], g5[:, :], w01[:, 0:1], None, ALU.mult),
                        [B(g5), bx(w01, 0, 128, 0, 1)], [B(g5)])
                    vop(lambda e: e.tensor_tensor(xa[:, :], g5[:, :], hww[:, :], ALU.add),
                        [B(g5), B(hww)], [B(xa)])
                    cv2 = vop(lambda e: e.tensor_scalar(
                        g5[:, :], xa[:, :], rsibc[:, 0:1], None, ALU.mult),
                        [B(xa), bx(rsibc, 0, 128, 0, 1)], [B(g5)], inc=("CV", 1))
                    P.wait("scalar", "CV", cv2)
                    cs3 = abs_sign_ln(g5[:, :], B(g5))
                    P.wait("vector", "CS", cs3)
                    cv3 = exp_chain(506, 515)
                    P.wait("scalar", "CV", cv3)
                    cs4 = exp_pair()
                    P.wait("vector", "CS", cs4)
                    vop(lambda e: e.tensor_copy(hw8[:, :], g5[:, :]), [B(g5)], [B(hw8)])
                    vop(lambda e: e.tensor_copy(hww[:, :], hw8[:, :]), [B(hw8)], [B(hww)])
                    vop(lambda e: e.tensor_tensor(g0[:, :], gax[:, :], g2[:, :], ALU.mult),
                        [B(gax), B(g2)], [B(g0)])
                    vop(lambda e: e.tensor_scalar(
                        g0[:, :], g0[:, :], B3S, B3S, ALU.add, ALU.subtract),
                        [B(g0)], [B(g0)])
                    vop(lambda e: e.tensor_scalar(
                        g1[:, :], g0[:, :], B3, B3, ALU.add, ALU.subtract),
                        [B(g0)], [B(g1)])
                    vop(lambda e: e.tensor_tensor(g0[:, :], g1[:, :], g3[:, :], ALU.mult),
                        [B(g1), B(g3)], [B(g0)])
                    vop(lambda e: e.tensor_tensor(g0[:, :], g0[:, :], gsg[:, :], ALU.mult),
                        [B(g0), B(gsg)], [B(g0)])
                    vop(lambda e: e.tensor_tensor(g1[:, :], g0[:, :], hww[:, :], ALU.subtract),
                        [B(g0), B(hww)], [B(g1)])
                    vop(lambda e: e.tensor_scalar(
                        g1[:, :], g1[:, :], w01[:, 0:1], None, ALU.mult),
                        [B(g1), bx(w01, 0, 128, 0, 1)], [B(g1)])
                    cv_yq = vop(lambda e: e.tensor_tensor(
                        yq[:, :], g1[:, :], hww[:, :], ALU.add),
                        [B(g1), B(hww)], [B(yq)], inc=("CV", 1))
                    P.wait("tensor", "CV", cv_yq)
                    P.wait("tensor", "CV", cv_xqT_prev)
                    P.wait("tensor", "CS", cs_y_prev)
                    tl = [(psA[:, k * 128:(k + 1) * 128],
                           yq[:, k * 128:(k + 1) * 128], idf[:, :]) for k in range(16)]
                    ct_tr = P.trans(tl)
                    P.wait("vector", "CT", ct_tr)
                    cv_xqT = P.op("vector", lambda e: e.tensor_copy(xqT[:, :], psA[:, :]),
                                  writes=[B(xqT)], inc=("CV", 1))
                    cv_xqT_prev = cv_xqT
                    P.wait("tensor", "CV", cv_xqT)
                    P.wait("tensor", "CV", kq_done)
                    mml = []
                    for tcl in range(2):
                        for dc in range(DC):
                            for c0 in (0, 512):
                                mml.append((psB[:, tcl * 1024 + c0:tcl * 1024 + c0 + 512],
                                            xqT[:, tcl * 1024 + dc * 128:tcl * 1024 + dc * 128 + 128],
                                            kq[:, dc * 1024 + c0:dc * 1024 + c0 + 512],
                                            dc == 0, dc == DC - 1))
                    ct_mm = P.mms(mml)
                    P.wait("scalar", "CT", ct_mm)
                    cs_y = P.op("scalar", lambda e: e.activation(
                        yq[:, :], psB[:, :], AF.Identity, bias=0.0, scale=ssbc[:, 0:1]),
                        reads=[bx(ssbc, 0, 128, 0, 1)], writes=[B(yq)], inc=("CS", 1))
                    cs_y_prev = cs_y
                    cs5 = abs_sign_ln(yq[:, :], B(yq))
                    P.wait("vector", "CS", cs5)
                    vop(lambda e: e.tensor_copy(hwb[:, :], yq[:, :]), [B(yq)], [B(hwb)])
                    vop(lambda e: e.tensor_copy(hww[:, :], hwb[:, :]), [B(hwb)], [B(hww)])
                    cv4 = exp_chain(440, 519)
                    P.wait("scalar", "CV", cv4)
                    cs6 = exp_pair()
                    P.wait("vector", "CS", cs6)
                    vop(lambda e: e.tensor_tensor(g0[:, :], gax[:, :], g2[:, :], ALU.mult),
                        [B(gax), B(g2)], [B(g0)])
                    vop(lambda e: e.tensor_scalar(
                        g1[:, :], g0[:, :], B3, B3, ALU.add, ALU.subtract),
                        [B(g0)], [B(g1)])
                    vop(lambda e: e.tensor_scalar(
                        g2[:, :], g0[:, :], B5, B5, ALU.add, ALU.subtract),
                        [B(g0)], [B(g2)])
                    vop(lambda e: e.tensor_tensor(g4[:, :], g0[:, :], g1[:, :], ALU.subtract),
                        [B(g0), B(g1)], [B(g4)])
                    vop(lambda e: e.tensor_tensor(g4[:, :], g4[:, :], g4[:, :], ALU.mult),
                        [B(g4)], [B(g4)])
                    vop(lambda e: e.tensor_scalar(
                        g4[:, :], g4[:, :], TIE_THR, None, ALU.is_ge),
                        [B(g4)], [B(g4)])
                    vop(lambda e: e.tensor_tensor(g2[:, :], g2[:, :], g1[:, :], ALU.subtract),
                        [B(g2), B(g1)], [B(g2)])
                    vop(lambda e: e.tensor_tensor(g2[:, :], g2[:, :], g4[:, :], ALU.mult),
                        [B(g2), B(g4)], [B(g2)])
                    vop(lambda e: e.tensor_tensor(g1[:, :], g1[:, :], g2[:, :], ALU.add),
                        [B(g1), B(g2)], [B(g1)])
                    vop(lambda e: e.tensor_tensor(g0[:, :], g1[:, :], g3[:, :], ALU.mult),
                        [B(g1), B(g3)], [B(g0)])
                    vop(lambda e: e.tensor_tensor(g0[:, :], g0[:, :], gsg[:, :], ALU.mult),
                        [B(g0), B(gsg)], [B(g0)])
                    vop(lambda e: e.tensor_tensor(g1[:, :], g0[:, :], hww[:, :], ALU.subtract),
                        [B(g0), B(hww)], [B(g1)])
                    vop(lambda e: e.tensor_scalar(
                        g1[:, :], g1[:, :], w01[:, 0:1], None, ALU.mult),
                        [B(g1), bx(w01, 0, 128, 0, 1)], [B(g1)])
                    vop(lambda e: e.tensor_tensor(g1[:, :], g1[:, :], hww[:, :], ALU.add),
                        [B(g1), B(hww)], [B(g1)])
                    vop(lambda e: e.tensor_tensor(g1[:, :], g1[:, :], bl2[:, :], ALU.add),
                        [B(g1), B(bl2)], [B(g1)])
                    cv_res_prev = vop(lambda e, X=X: e.tensor_tensor(X, X, g1[:, :], ALU.add),
                                      [Xb, B(g1)], [Xb], inc=("CV", 1))
                return ln_inplace(P.cnt["CV"])

        last_cv = 0
        with contextlib.ExitStack() as SL:
            xTf = SL.enter_context(nc.sbuf_tensor("xTf0", [128, DC * T], F32))
            build_xT_f32(xTf, 0)
            last_cv = attn_layer(0, xTf, 0)
        if parts >= 2:
            with contextlib.ExitStack() as SL:
                xT_hi = SL.enter_context(nc.sbuf_tensor("xT_hi", [128, DC * T], BF16))
                xT_lo = SL.enter_context(nc.sbuf_tensor("xT_lo", [128, DC * T], BF16))
                build_xT(xT_hi, xT_lo, last_cv, alt=False)
                gates = {"cv": P.cnt["CV"], "cs": P.cnt["CS"], "lw": P.cnt["LW"]}
                with contextlib.ExitStack() as TSK:
                    tern_st = ternarize_weights(TSK)
                    last_cv = ffn0_layer(xT_hi, xT_lo, tern_st, gates)
        if parts >= 3:
            with contextlib.ExitStack() as SL:
                xTf = SL.enter_context(nc.sbuf_tensor("xTf1", [128, DC * T], F32))
                build_xT_f32(xTf, last_cv)
                last_cv = attn_layer(1, xTf, last_cv)
        if parts >= 4:
            last_cv = ffn1_layer()

        for tc in range(TC):
            P.dma(bass.AP(out_t, tc * 128 * D, [[D, 128], [1, D]]),
                  x_res[:, tc * D:(tc + 1) * D], sem="ST", wait=("CV", last_cv))
        P.wait("sync", "ST", P.cnt["ST"])

        with nc.Block() as block:
            @block.sync
            def _(e):
                for f in P.ops["sync"]:
                    f(e)

            @block.vector
            def _(e):
                for f in P.ops["vector"]:
                    f(e)

            @block.scalar
            def _(e):
                for f in P.ops["scalar"]:
                    f(e)

            @block.tensor
            def _(e):
                for f in P.ops["tensor"]:
                    f(e)

            @block.gpsimd
            def _(e):
                for f in P.ops["gpsimd"]:
                    f(e)

    return nc


def _consts():
    r = np.arange(128)
    diag = np.where(r[:, None] > r[None, :], np.float32(-8e9), np.float32(0.0)).astype(np.float32)
    return {
        "ident_f": np.eye(128, dtype=np.float32),
        "ident_b": np.eye(128, dtype=np.float32).astype(ml_dtypes.bfloat16),
        "diagm": diag,
    }


def kernel(**inputs):
    nc = build_program()
    consts = _consts()
    weights = {k: np.ascontiguousarray(v) for k, v in inputs.items()
               if k != "L0_attn_embeddings"}
    x = np.ascontiguousarray(inputs["L0_attn_embeddings"])
    in_maps = [dict(weights, **consts, x_in=np.ascontiguousarray(x[i])) for i in range(8)]
    res = run_bass_kernel_spmd(nc, in_maps, core_ids=list(range(8)))
    return np.stack([np.asarray(r["out"]) for r in res.results], axis=0)
